# version 16
# speedup vs baseline: 1.1340x; 1.0083x over previous
import numpy as np
from contextlib import ExitStack

import concourse.bass as bass
import concourse.mybir as mybir
from concourse.bass_utils import run_bass_kernel_spmd

F32 = mybir.dt.float32
BF16 = mybir.dt.bfloat16
U32 = mybir.dt.uint32
AF = mybir.ActivationFunctionType
ALU = mybir.AluOpType
AX = mybir.AxisListType.X

NCORES = 8
T = 2048
D = 1024
PT = 8
EPS = 1e-5
MAGIC = 12582912.0
TWO_PI = float(2 * np.pi)
C1 = float(np.float32(TWO_PI))
C2 = float(TWO_PI - C1)
INV2PI = float(1.0 / TWO_PI)

PHASES = 6
DEBUG_TAPS = False
DEBUG_TAP_TAG = 1


class G:
    pass


def inc(g, name, ins, by=1):
    ins.then_inc(g.sem[name], by)
    g.cnt[name] += by
    return g.cnt[name]


MAX_TBL_RUNS = 1


def sdefer(g, n, build, tbl='c'):
    if g.sc_pend and g.sc_last != tbl and g.sc_runs >= MAX_TBL_RUNS:
        sflush(g)
    if not g.sc_pend:
        g.sc_runs = 1
    elif g.sc_last != tbl:
        g.sc_runs += 1
    g.sc_last = tbl
    g.cnt["sd"] += n
    g.sc_pend.append(build)
    return g.cnt["sd"]


def sflush(g):
    if not g.sc_pend:
        return
    items = g.sc_pend
    g.sc_pend = []
    g.sc_runs = 0
    g.sc_last = None

    def f(s):
        for it in items:
            it(s)
    g.block.scalar(f)


def join(g):
    sflush(g)
    snap = {k: v for k, v in g.cnt.items() if v > 0}

    def mk(eng):
        for s, v in snap.items():
            eng.wait_ge(g.sem[s], v)

    b = g.block
    b.sync(mk); b.vector(mk); b.scalar(mk); b.tensor(mk); b.gpsimd(mk)


def emit_ln(g, lns, srcs, outs, W, gamma=None, beta=None, pso=0, entry_vd=None):
    b, sem, cnt = g.block, g.sem, g.cnt
    full = W > 512
    if full:
        reg_s1 = lambda lo, w: g.psA[:, lo:lo + w]
        reg_s2 = lambda lo, w: g.psB[:, lo:lo + w]
        reg_bi, reg_bm = reg_s1, reg_s2
        row_s1 = g.psA[0:1, 0:W]
        row_s2 = g.psB[0:1, 0:W]
    else:
        reg_s1 = lambda lo, w: g.psA[:, pso + lo:pso + lo + w]
        reg_s2 = lambda lo, w: g.psA[:, pso + 256 + lo:pso + 256 + lo + w]
        reg_bi = reg_s1
        reg_bm = reg_s2
        row_s1 = g.psA[0:1, pso:pso + W]
        row_s2 = g.psA[0:1, pso + 256:pso + 256 + W]
    nchk = (W + 511) // 512
    cw = [min(512, W - i * 512) for i in range(nchk)]

    sd0 = cnt["sd"]
    td0 = cnt["td"]
    vd_entry = cnt["vd"] if entry_vd is None else entry_vd

    def b_copy(s):
        s.wait_ge(sem["vd"], vd_entry)
        for pt in range(PT):
            if pt >= 2:
                s.wait_ge(sem["td"], td0 + pt - 1)
            s.copy(out=lns['lnA'][pt % 2][:, 0:W],
                   in_=srcs[pt]).then_inc(sem["sd"], 1)
    sdefer(g, PT, b_copy, 'c')

    vsq0 = cnt["vd"]

    def f_vsq(v):
        v.wait_ge(sem["vd"], vd_entry)
        for pt in range(PT):
            if pt >= 2:
                v.wait_ge(sem["td"], td0 + pt - 1)
            inc(g, "vd", v.tensor_tensor(out=lns['lnB'][pt % 2][:, 0:W],
                                         in0=srcs[pt], in1=srcs[pt],
                                         op=ALU.mult))
    b.vector(f_vsq)

    def f_pe(t):
        t.wait_ge(sem["vd"], vd_entry)
        for pt in range(PT):
            t.wait_ge(sem["sd"], sd0 + pt + 1)
            t.wait_ge(sem["vd"], vsq0 + pt + 1)
            last = None
            for i in range(nchk):
                lo = i * 512
                t.matmul(reg_s1(lo, cw[i]), g.ones_bf[:, :],
                         lns['lnA'][pt % 2][:, lo:lo + cw[i]],
                         start=(pt == 0), stop=(pt == PT - 1))
                last = t.matmul(reg_s2(lo, cw[i]), g.ones_bf[:, :],
                                lns['lnB'][pt % 2][:, lo:lo + cw[i]],
                                start=(pt == 0), stop=(pt == PT - 1))
            inc(g, "td", last)
    b.tensor(f_pe)
    td_stats = cnt["td"]

    def f_v1(v):
        def settle(ins):
            v.wait_ge(sem["vd"], inc(g, "vd", ins))
        v.wait_ge(sem["td"], td_stats)
        settle(v.tensor_scalar(out=lns['row_m'][:, 0:W], in0=row_s1,
                               scalar1=1.0 / D, scalar2=None, op0=ALU.mult))
        settle(v.tensor_scalar(out=lns['row_v'][:, 0:W], in0=row_s2,
                               scalar1=1.0 / D, scalar2=EPS,
                               op0=ALU.mult, op1=ALU.add))
        settle(v.tensor_tensor(out=lns['lnT'][0:1, 0:W], in0=lns['row_m'][:, 0:W],
                               in1=lns['row_m'][:, 0:W], op=ALU.mult))
        settle(v.tensor_tensor(out=lns['row_v'][:, 0:W], in0=lns['row_v'][:, 0:W],
                               in1=lns['lnT'][0:1, 0:W], op=ALU.subtract))
        inc(g, "vd", v.reciprocal(out=lns['row_v'][:, 0:W],
                                  in_=lns['row_v'][:, 0:W]))
    b.vector(f_v1)
    vd2 = cnt["vd"]

    def b_s3(s):
        s.wait_ge(sem["vd"], vd2)
        s.activation(out=lns['row_v'][:, 0:W], in_=lns['row_v'][:, 0:W],
                     func=AF.Sqrt).then_inc(sem["sd"], 1)
    sd3 = sdefer(g, 1, b_s3, 'sqrt')

    def f_v3(v):
        v.wait_ge(sem["sd"], sd3)
        inc(g, "vd", v.tensor_tensor(out=lns['row_m'][:, 0:W], in0=lns['row_m'][:, 0:W],
                                     in1=lns['row_v'][:, 0:W], op=ALU.mult))
    b.vector(f_v3)
    vd3 = cnt["vd"]

    def f_pe2(t):
        t.wait_ge(sem["vd"], vd3)
        last = None
        for i in range(nchk):
            lo = i * 512
            t.matmul(reg_bi(lo, cw[i]), g.ones1_f32[:, :],
                     lns['row_v'][:, lo:lo + cw[i]], start=True, stop=True)
            last = t.matmul(reg_bm(lo, cw[i]), g.ones1_f32[:, :],
                            lns['row_m'][:, lo:lo + cw[i]], start=True, stop=True)
        inc(g, "td", last)
    b.tensor(f_pe2)
    td_bc = cnt["td"]

    def f_v4(v):
        v.wait_ge(sem["td"], td_bc)
        last = None
        for pt in range(PT):
            v.tensor_tensor(out=lns['lnT'][:, 0:W], in0=srcs[pt], in1=reg_bi(0, W),
                            op=ALU.mult)
            last = v.tensor_tensor(out=outs[pt], in0=lns['lnT'][:, 0:W],
                                   in1=reg_bm(0, W), op=ALU.subtract)
            if gamma is not None:
                last = v.tensor_scalar(out=outs[pt], in0=outs[pt],
                                       scalar1=gamma[pt], scalar2=beta[pt],
                                       op0=ALU.mult, op1=ALU.add)
        inc(g, "vd", last)
    b.vector(f_v4)


def emit_gating(g, t0, gw_sb, mode):
    b, sem, cnt = g.block, g.sem, g.cnt
    vd_entry = cnt["vd"]
    dg_entry = cnt["dg"]

    def f_pe(t):
        t.wait_ge(sem["vd"], vd_entry)
        t.wait_ge(sem["dg"], dg_entry)
        last = None
        for ch in range(16):
            reg = g.psA[:, ch * 8:ch * 8 + 8]
            for kt in range(PT):
                last = t.matmul(reg, t0[kt][:, ch * 128:(ch + 1) * 128],
                                gw_sb[:, kt * 8:(kt + 1) * 8],
                                start=(kt == 0),
                                stop=(kt == 7 and mode == "l1"))
            if mode == "l0":
                last = t.matmul(reg, g.ones_bf[0:1, :], g.gbrow_bf[:, :],
                                start=False, stop=True)
        inc(g, "td", last)
    b.tensor(f_pe)
    td_log = cnt["td"]

    if mode == "l0" and DEBUG_TAPS and g.tag == DEBUG_TAP_TAG:
        def f_tapv(v):
            v.wait_ge(sem["td"], td_log)
            inc(g, "vd", v.tensor_copy(out=g.gis1[:, :], in_=g.psA[:, 0:128]))
        b.vector(f_tapv)
        vd_tap = cnt["vd"]
        def f_tapd(q):
            q.wait_ge(sem["vd"], vd_tap)
            inc(g, "ds", q.dma_start(out=g.tap_lg, in_=g.gis1[:, :]), 16)
        b.sync(f_tapd)

    if mode == "l0":
        def f_v0(v):
            v.wait_ge(sem["td"], td_log)
            last = None
            for ch in range(16):
                last = v.tensor_reduce(out=g.gnm[:, ch:ch + 1],
                                       in_=g.psA[:, ch * 8:ch * 8 + 8],
                                       axis=AX, op=ALU.max, negate=True)
            inc(g, "vd", last)
        b.vector(f_v0)
        vd_nm = cnt["vd"]

        def b_s(s):
            s.wait_ge(sem["vd"], vd_nm)
            for ch in range(16):
                last = s.activation(out=g.gsc[:, ch * 8:ch * 8 + 8],
                                    in_=g.psA[:, ch * 8:ch * 8 + 8],
                                    func=AF.Exp, bias=g.gnm[:, ch:ch + 1])
            last.then_inc(sem["sd"], 1)
        sd_exp = sdefer(g, 1, b_s, 'exp')

        if DEBUG_TAPS and g.tag == DEBUG_TAP_TAG:
            def f_tapd2(q):
                q.wait_ge(sem["sd"], sd_exp)
                inc(g, "ds", q.dma_start(out=g.tap_nm, in_=g.gnm[:, :]), 16)
                inc(g, "ds", q.dma_start(out=g.tap_ex, in_=g.gsc[:, :]), 16)
            b.sync(f_tapd2)

        def f_v2(v):
            def settle(ins):
                v.wait_ge(sem["vd"], inc(g, "vd", ins))
            v.wait_ge(sem["sd"], sd_exp)
            last = None
            for ch in range(16):
                e = g.gsc[:, ch * 8:ch * 8 + 8]
                settle(v.tensor_reduce(out=g.colA[:, :], in_=e, axis=AX,
                                       op=ALU.add))
                settle(v.tensor_scalar(out=g.colB[:, :], in0=g.colA[:, :],
                                       scalar1=0.1, scalar2=None, op0=ALU.mult))
                settle(v.tensor_scalar(out=g.gs2[:, ch * 8:ch * 8 + 8], in0=e,
                                       scalar1=g.colB[:, :], op0=ALU.is_ge,
                                       scalar2=None))
                settle(v.tensor_tensor(out=g.gs2[:, ch * 8:ch * 8 + 8], in0=e,
                                       in1=g.gs2[:, ch * 8:ch * 8 + 8],
                                       op=ALU.mult))
                settle(v.tensor_reduce(out=g.colA[:, :],
                                       in_=g.gs2[:, ch * 8:ch * 8 + 8], axis=AX,
                                       op=ALU.add))
                settle(v.reciprocal(out=g.colA[:, :], in_=g.colA[:, :]))
                last = v.tensor_scalar(out=g.gatw[:, ch * 8:ch * 8 + 8],
                                       in0=g.gs2[:, ch * 8:ch * 8 + 8],
                                       scalar1=g.colA[:, :], scalar2=None,
                                       op0=ALU.mult)
            inc(g, "vd", last)
        b.vector(f_v2)
    else:
        def f_v1(v):
            v.wait_ge(sem["td"], td_log)
            last = None
            def settle(ins):
                v.wait_ge(sem["vd"], inc(g, "vd", ins))
            for ch in range(16):
                lg = g.psA[:, ch * 8:ch * 8 + 8]
                settle(v.tensor_reduce(out=g.colA[:, :], in_=lg, axis=AX,
                                       op=ALU.max))
                settle(v.tensor_scalar(out=g.gis1[:, ch * 8:ch * 8 + 8], in0=lg,
                                       scalar1=g.colA[:, :], op0=ALU.is_ge,
                                       scalar2=None))
                settle(v.scalar_tensor_tensor(out=g.gs2[:, ch * 8:ch * 8 + 8],
                                              in0=g.gis1[:, ch * 8:ch * 8 + 8],
                                              scalar=-1.0e9, in1=lg,
                                              op0=ALU.mult, op1=ALU.add))
                settle(v.tensor_reduce(out=g.colB[:, :],
                                       in_=g.gs2[:, ch * 8:ch * 8 + 8], axis=AX,
                                       op=ALU.max))
                settle(v.tensor_scalar(out=g.gis2[:, ch * 8:ch * 8 + 8],
                                       in0=g.gs2[:, ch * 8:ch * 8 + 8],
                                       scalar1=g.colB[:, :], op0=ALU.is_ge,
                                       scalar2=None))
                last = v.tensor_tensor(out=g.colC[:, ch:ch + 1],
                                       in0=g.colA[:, :], in1=g.colB[:, :],
                                       op=ALU.subtract)
            inc(g, "vd", last)
        b.vector(f_v1)
        vd_pre = cnt["vd"]

        def b_sg(s):
            s.wait_ge(sem["vd"], vd_pre)
            s.activation(out=g.colP[:, 0:16], in_=g.colC[:, 0:16],
                         func=AF.Sigmoid).then_inc(sem["sd"], 1)
        sd_sig = sdefer(g, 1, b_sg, 'sig')

        def f_v2(v):
            v.wait_ge(sem["sd"], sd_sig)
            last = None
            def settle(ins):
                v.wait_ge(sem["vd"], inc(g, "vd", ins))
            for ch in range(16):
                p1 = g.colP[:, ch:ch + 1]
                settle(v.tensor_scalar(out=g.colB[:, :], in0=p1, scalar1=-1.0,
                                       scalar2=1.0, op0=ALU.mult, op1=ALU.add))
                settle(v.tensor_scalar(out=g.gs2[:, ch * 8:ch * 8 + 8],
                                       in0=g.gis2[:, ch * 8:ch * 8 + 8],
                                       scalar1=g.colB[:, :], scalar2=None,
                                       op0=ALU.mult))
                last = v.scalar_tensor_tensor(out=g.gatw[:, ch * 8:ch * 8 + 8],
                                              in0=g.gis1[:, ch * 8:ch * 8 + 8],
                                              scalar=p1,
                                              in1=g.gs2[:, ch * 8:ch * 8 + 8],
                                              op0=ALU.mult, op1=ALU.add)
            inc(g, "vd", last)
        b.vector(f_v2)

    vd_w = cnt["vd"]

    def f_pe2(t):
        t.wait_ge(sem["vd"], vd_w)
        last = None
        for ch in range(16):
            last = t.transpose(g.psB[0:8, ch * 128:(ch + 1) * 128],
                               g.gatw[:, ch * 8:ch * 8 + 8], g.eye32[:, :])
        inc(g, "td", last)
    b.tensor(f_pe2)
    td_tr = cnt["td"]

    def b_s2(s):
        s.wait_ge(sem["td"], td_tr)
        s.copy(out=g.wT8[:, :], in_=g.psB[0:8, 0:2048]).then_inc(sem["sd"], 1)
    sd_w = sdefer(g, 1, b_s2, 'c')

    g.sd_wT8 = cnt["sd"]

    if DEBUG_TAPS and g.tag == DEBUG_TAP_TAG:
        def f_tapg(q):
            q.wait_ge(sem["sd"], g.sd_wT8)
            q.wait_ge(sem["vd"], cnt["vd"])
            inc(g, "ds", q.dma_start(out=g.tap_gw, in_=g.gatw[:, :]), 16)
            inc(g, "ds", q.dma_start(out=g.tap_wT8, in_=g.wT8[:, :]), 16)
        b.sync(f_tapg)


def alloc_ln(g, ph, W, tag, shared_lnT=None):
    nc = g.nc
    sbc = lambda nm, sh, dt=F32: ph.enter_context(nc.sbuf_tensor(nm, sh, dt))
    return {
        "lnA": [sbc(f"{tag}_lnA{i}", [128, W], BF16) for i in range(2)],
        "lnB": [sbc(f"{tag}_lnB{i}", [128, W], BF16) for i in range(2)],
        "lnT": shared_lnT if shared_lnT is not None else sbc(f"{tag}_lnT", [128, W]),
        "row_m": sbc(f"{tag}_rm", [1, W]),
        "row_v": sbc(f"{tag}_rv", [1, W]),
    }


def emit_l0ffn(g):
    b, sem, cnt, nc = g.block, g.sem, g.cnt, g.nc
    g.tag += 1
    tg = g.tag
    with ExitStack() as ph:
        g.wT8 = ph.enter_context(nc.sbuf_tensor(f"ph{tg}_wT8", [8, 2048], BF16))
        t0 = [ph.enter_context(nc.sbuf_tensor(f"ph{tg}_t0_{i}", [128, T], BF16))
              for i in range(PT)]
        t0e = [ph.enter_context(nc.sbuf_tensor(f"ph{tg}_te_{i}", [128, 512], BF16))
               for i in range(PT)]
        wsl = [ph.enter_context(nc.sbuf_tensor(f"ph{tg}_w_{i}", [128, 8192], BF16))
               for i in range(2)]

        with ExitStack() as lnph:
            lns = alloc_ln(g, lnph, T, f"ph{tg}ln")
            emit_ln(g, lns, [g.xs[pt][:, :] for pt in range(PT)],
                    [t0[pt][:, :] for pt in range(PT)], T)
            emit_gating(g, t0, g.l0gw_sb, "l0")

            if DEBUG_TAPS and tg == DEBUG_TAP_TAG:
                vd_now = cnt["vd"]; sd_now = cnt["sd"]
                def f_tap(q):
                    q.wait_ge(sem["vd"], vd_now)
                    q.wait_ge(sem["sd"], sd_now)
                    for pt in range(PT):
                        inc(g, "ds", q.dma_start(out=g.tap_t0[pt * 128:(pt + 1) * 128, :],
                                                 in_=t0[pt][:, :]), 16)
                    inc(g, "ds", q.dma_start(out=g.tap_gw, in_=g.gatw[:, :]), 16)
                    inc(g, "ds", q.dma_start(out=g.tap_wT8, in_=g.wT8[:, :]), 16)
                b.sync(f_tap)

            mm_td = {}
            ev_vd = {}
            mm_done = {}
            scale_vd = {}
            scale0 = {}
            bc_td = {}
            row_dg = {}

            gidx_of = lambda e, n, mt: (e * 4 + n) * 8 + mt

            for e in range(8):
                def f_dma(q, e=e):
                    if e >= 2:
                        q.wait_ge(sem["td"], mm_done[e - 2])
                    row_dg[e] = inc(
                        g, "dg", q.dma_start(out=wsl[e % 2][:, :],
                                             in_=g.l0f_w[e * 128:(e + 1) * 128, :]), 16)
                b.gpsimd(f_dma)

                def f_bc(t, e=e):
                    if e == 0:
                        t.wait_ge(sem["vd"], cnt["vd"])
                        t.wait_ge(sem["sd"], g.sd_wT8)
                    else:
                        t.wait_ge(sem["vd"], scale_vd[(e - 1, 3)])
                    last = None
                    for i in range(4):
                        last = t.matmul(
                            g.psA[:, i * 512:(i + 1) * 512],
                            g.oh8_bf[:, e * 128:(e + 1) * 128],
                            g.wT8[:, i * 512:(i + 1) * 512],
                            start=True, stop=True)
                    bc_td[e] = inc(g, "td", last)
                b.tensor(f_bc)

                for n in range(4):
                    scale0[(e, n)] = cnt["vd"]

                    def f_sc(v, e=e, n=n):
                        v.wait_ge(sem["td"], bc_td[e])
                        if e > 0 or n > 0:
                            pe, pn = (e, n - 1) if n > 0 else (e - 1, 3)
                            v.wait_ge(sem["td"], mm_td[gidx_of(pe, pn, 7)])
                        for kt in range(PT):
                            inc(g, "vd", v.tensor_tensor(
                                out=t0e[kt][:, :],
                                in0=t0[kt][:, n * 512:(n + 1) * 512],
                                in1=g.psA[:, n * 512:(n + 1) * 512], op=ALU.mult))
                        scale_vd[(e, n)] = cnt["vd"]
                    b.vector(f_sc)

                    for mt in range(PT):
                        def f_mm(t, e=e, n=n, mt=mt):
                            gidx = gidx_of(e, n, mt)
                            slot = gidx % 4
                            reg = g.psB[:, slot * 512:(slot + 1) * 512]
                            if mt == 0:
                                t.wait_ge(sem["dg"], row_dg[e])
                            if gidx >= 4:
                                t.wait_ge(sem["vd"], ev_vd[gidx - 4])
                            nb = (e == 0)
                            last = None
                            for kt in range(PT):
                                if mt == 0:
                                    t.wait_ge(sem["vd"], scale0[(e, n)] + kt + 1)
                                last = t.matmul(
                                    reg,
                                    wsl[e % 2][:, (kt * 8 + mt) * 128:(kt * 8 + mt + 1) * 128],
                                    t0e[kt][:, :],
                                    start=(kt == 0), stop=(kt == 7 and not nb))
                            if nb:
                                last = t.matmul(reg,
                                                g.l0fb_bf[:, mt * 128:(mt + 1) * 128],
                                                g.wT8[:, n * 512:(n + 1) * 512],
                                                start=False, stop=True)
                            mm_td[gidx_of(e, n, mt)] = inc(g, "td", last)
                        b.tensor(f_mm)

                        def f_ev(v, e=e, n=n, mt=mt):
                            gidx = gidx_of(e, n, mt)
                            slot = gidx % 4
                            v.wait_ge(sem["td"], mm_td[gidx])
                            last = v.scalar_tensor_tensor(
                                out=g.xs[mt][:, n * 512:(n + 1) * 512],
                                in0=g.psB[:, slot * 512:(slot + 1) * 512],
                                scalar=0.5,
                                in1=g.xs[mt][:, n * 512:(n + 1) * 512],
                                op0=ALU.mult, op1=ALU.add)
                            ev_vd[gidx] = inc(g, "vd", last)
                        b.vector(f_ev)
                mm_done[e] = cnt["td"]
        join(g)



def emit_l0attn(g):
    b, sem, cnt, nc = g.block, g.sem, g.cnt, g.nc
    g.tag += 1
    tg = g.tag
    W = 256
    with ExitStack() as ph:
        sbc = lambda nm, sh, dt=F32: ph.enter_context(nc.sbuf_tensor(f"ph{tg}_{nm}", sh, dt))
        w1t = sbc("w1t", [128, 2048], BF16)
        w2t = sbc("w2t", [128, 32 * 1024], BF16)
        tc2 = [[sbc(f"tc{p}_{i}", [128, W]) for i in range(PT)] for p in range(2)]
        x1c2 = [[sbc(f"x1c{p}_{i}", [128, W]) for i in range(PT)] for p in range(2)]
        hc = [sbc(f"hc{i}", [128, 1024], BF16) for i in range(PT)]
        qa = sbc("qa", [128, 1024]); qb = sbc("qb", [128, 1024])
        qc = qb
        qd = qa
        qf2 = [sbc(f"qf{p}", [128, W], BF16) for p in range(2)]
        lns2 = [alloc_ln(g, ph, W, f"ph{tg}ln0")]
        lns2.append(alloc_ln(g, ph, W, f"ph{tg}ln1", shared_lnT=lns2[0]["lnT"]))

        def f_wdma(q):
            inc(g, "dg", q.dma_start(out=w1t[:, :], in_=g.l0a_w1t), 16)
            inc(g, "dg", q.dma_start(out=w2t[:, :], in_=g.l0a_w2t), 16)
        b.gpsimd(f_wdma)
        dg_w = cnt["dg"]

        hmm_td = {}
        hev_sd = {}
        fmm_td = {}
        fev_vd = {}
        vd_qf = {}

        def emit_front(ci):
            par = ci % 2
            pso = par * 512
            c0 = ci * W
            tc = tc2[par]
            x1c = x1c2[par]
            qf = qf2[par]
            lns = lns2[par]
            xsrc = [g.xs[pt][:, c0:c0 + W] for pt in range(PT)]
            emit_ln(g, lns, xsrc, [tc[pt][:, :] for pt in range(PT)], W, pso=pso)

            for tt in range(2):
                lo = tt * 128
                def f_tr(t, tt=tt, lo=lo):
                    t.wait_ge(sem["vd"], cnt["vd"])
                    last = None
                    for pt in range(PT):
                        last = t.transpose(g.psB[:, 1024 + pt * 128:1024 + (pt + 1) * 128],
                                           tc[pt][:, lo:lo + 128], g.eye32[:, :])
                    inc(g, "td", last)
                b.tensor(f_tr)
                td_tr = cnt["td"]

                def f_rr(v, td_tr=td_tr):
                    def settle(ins):
                        v.wait_ge(sem["vd"], inc(g, "vd", ins))
                    v.wait_ge(sem["td"], td_tr)
                    settle(v.tensor_scalar(out=qa[:, :], in0=g.psB[:, 1024:2048],
                                           scalar1=INV2PI, scalar2=0.25,
                                           op0=ALU.mult, op1=ALU.add))
                    settle(v.tensor_scalar(out=qa[:, :], in0=qa[:, :],
                                           scalar1=MAGIC, scalar2=MAGIC,
                                           op0=ALU.add, op1=ALU.subtract))
                    settle(v.scalar_tensor_tensor(out=qb[:, :], in0=qa[:, :],
                                                  scalar=-C1, in1=g.psB[:, 1024:2048],
                                                  op0=ALU.mult, op1=ALU.add))
                    settle(v.scalar_tensor_tensor(out=qb[:, :], in0=qa[:, :],
                                                  scalar=-C2, in1=qb[:, :],
                                                  op0=ALU.mult, op1=ALU.add))
                b.vector(f_rr)
                vd_rr = cnt["vd"]

                def b_cos(s, vd_rr=vd_rr):
                    s.wait_ge(sem["vd"], vd_rr)
                    s.activation(out=qc[:, :], in_=qb[:, :],
                                 func=AF.Sin, bias=g.halfpi[:, :]).then_inc(sem["sd"], 1)
                sd_cos = sdefer(g, 1, b_cos, 'sin')

                def f_scan(v, sd_cos=sd_cos):
                    v.wait_ge(sem["sd"], sd_cos)
                    last = None
                    for h in range(16):
                        last = v.tensor_tensor_scan(
                            out=qd[:, h * 64:(h + 1) * 64],
                            data0=qc[:, h * 64:(h + 1) * 64],
                            data1=g.zero64[:, 0:64], initial=1.0,
                            op0=ALU.mult, op1=ALU.add)
                    inc(g, "vd", last)
                b.vector(f_scan)
                vd_scan = cnt["vd"]

                def f_tr2(t, vd_scan=vd_scan):
                    t.wait_ge(sem["vd"], vd_scan)
                    last = None
                    for pt in range(PT):
                        last = t.transpose(g.psB[:, 1024 + pt * 128:1024 + (pt + 1) * 128],
                                           qd[:, pt * 128:(pt + 1) * 128],
                                           g.eye32[:, :])
                    inc(g, "td", last)
                b.tensor(f_tr2)
                td_tr2 = cnt["td"]

                def f_add(v, tt=tt, lo=lo, td_tr2=td_tr2):
                    v.wait_ge(sem["td"], td_tr2)
                    last = None
                    for pt in range(PT):
                        last = v.tensor_tensor(
                            out=tc[pt][:, lo:lo + 128], in0=tc[pt][:, lo:lo + 128],
                            in1=g.psB[:, 1024 + pt * 128:1024 + (pt + 1) * 128],
                            op=ALU.add)
                    inc(g, "vd", last)
                b.vector(f_add)

            emit_ln(g, lns, [tc[pt][:, :] for pt in range(PT)],
                    [x1c[pt][:, :] for pt in range(PT)], W, pso=pso,
                    gamma=[g.g1c[:, pt:pt + 1] for pt in range(PT)],
                    beta=[g.bt1c[:, pt:pt + 1] for pt in range(PT)])

            for tt in range(2):
                lo = tt * 128
                def f_q2tr(t, lo=lo):
                    t.wait_ge(sem["vd"], cnt["vd"])
                    inc(g, "td", t.transpose(g.psB[:, 1024:1040],
                                             x1c[0][0:16, lo:lo + 128],
                                             g.eye32[0:16, 0:16]))
                b.tensor(f_q2tr)
                td_q = cnt["td"]

                def f_q2rr(v, td_q=td_q):
                    def settle(ins):
                        v.wait_ge(sem["vd"], inc(g, "vd", ins))
                    v.wait_ge(sem["td"], td_q)
                    settle(v.tensor_scalar(out=qa[:, 0:16], in0=g.psB[:, 1024:1040],
                                           scalar1=INV2PI, scalar2=0.25,
                                           op0=ALU.mult, op1=ALU.add))
                    settle(v.tensor_scalar(out=qa[:, 0:16], in0=qa[:, 0:16],
                                           scalar1=MAGIC, scalar2=MAGIC,
                                           op0=ALU.add, op1=ALU.subtract))
                    settle(v.scalar_tensor_tensor(out=qb[:, 0:16], in0=qa[:, 0:16],
                                                  scalar=-C1, in1=g.psB[:, 1024:1040],
                                                  op0=ALU.mult, op1=ALU.add))
                    settle(v.scalar_tensor_tensor(out=qb[:, 0:16], in0=qa[:, 0:16],
                                                  scalar=-C2, in1=qb[:, 0:16],
                                                  op0=ALU.mult, op1=ALU.add))
                b.vector(f_q2rr)
                vd_q = cnt["vd"]

                def b_q2cos(s, vd_q=vd_q):
                    s.wait_ge(sem["vd"], vd_q)
                    s.activation(out=qc[:, 0:16], in_=qb[:, 0:16],
                                 func=AF.Sin, bias=g.halfpi[:, :]).then_inc(sem["sd"], 1)
                sd_q = sdefer(g, 1, b_q2cos, 'sin')

                def f_q2scan(v, sd_q=sd_q):
                    v.wait_ge(sem["sd"], sd_q)
                    inc(g, "vd", v.tensor_tensor_scan(
                        out=qd[:, 0:16], data0=qc[:, 0:16],
                        data1=g.zero64[:, 0:16], initial=1.0,
                        op0=ALU.mult, op1=ALU.add))
                b.vector(f_q2scan)
                vd_qs = cnt["vd"]

                def f_q2tr2(t, vd_qs=vd_qs):
                    t.wait_ge(sem["vd"], vd_qs)
                    inc(g, "td", t.transpose(g.psB[0:16, 1152:1280], qd[:, 0:16],
                                             g.eye32[:, :]))
                b.tensor(f_q2tr2)
                td_q2 = cnt["td"]

                def f_q2ev(v, ci=ci, lo=lo, td_q2=td_q2):
                    v.wait_ge(sem["td"], td_q2)
                    if ci >= 2 and lo == 0:
                        v.wait_ge(sem["td"], hmm_td[(ci - 2, 31)])
                    v.tensor_copy(out=qf[0:16, lo:lo + 128],
                                  in_=g.psB[0:16, 1152:1280])
                    inc(g, "vd", v.tensor_copy(out=qf[64:80, lo:lo + 128],
                                               in_=g.psB[0:16, 1152:1280]))
                b.vector(f_q2ev)
            vd_qf[ci] = cnt["vd"]

        def emit_hstage(ci):
            par = ci % 2
            qf = qf2[par]
            for mt in range(32):
                def f_hmm(t, ci=ci, mt=mt):
                    if mt == 0:
                        t.wait_ge(sem["vd"], vd_qf[ci])
                        t.wait_ge(sem["dg"], dg_w)
                    if mt >= 2:
                        t.wait_ge(sem["sd"], hev_sd[(ci, mt - 2)])
                    reg = g.psA[:, 1024 + (mt % 2) * 512:1024 + (mt % 2) * 512 + W]
                    hmm_td[(ci, mt)] = inc(g, "td", t.matmul(
                        reg,
                        w1t[(mt % 2) * 64:(mt % 2) * 64 + 16,
                            (mt // 2) * 128:(mt // 2) * 128 + 128],
                        qf[(mt % 2) * 64:(mt % 2) * 64 + 16, :],
                        start=True, stop=True))
                b.tensor(f_hmm)

                def b_hev(s, ci=ci, mt=mt):
                    s.wait_ge(sem["td"], hmm_td[(ci, mt)])
                    reg = g.psA[:, 1024 + (mt % 2) * 512:1024 + (mt % 2) * 512 + W]
                    s.activation(
                        out=hc[mt // 4][:, (mt % 4) * W:(mt % 4 + 1) * W],
                        in_=reg, func=AF.Relu,
                        bias=g.b1c[:, mt:mt + 1]).then_inc(sem["sd"], 1)
                hev_sd[(ci, mt)] = sdefer(g, 1, b_hev, 'c')

        def emit_tail(ci):
            par = ci % 2
            c0 = ci * W
            tc = tc2[par]
            x1c = x1c2[par]
            lns = lns2[par]
            for mto in range(PT):
                def f_fmm(t, ci=ci, mto=mto):
                    if mto == 0:
                        t.wait_ge(sem["sd"], hev_sd[(ci, 31)])
                    if mto >= 2:
                        t.wait_ge(sem["vd"], fev_vd[(ci, mto - 2)])
                    elif ci > 0:
                        t.wait_ge(sem["vd"], fev_vd[(ci - 1, 6 + mto)])
                    reg = g.psB[:, (mto % 2) * 512:(mto % 2) * 512 + W]
                    last = None
                    for kt in range(32):
                        last = t.matmul(
                            reg, w2t[:, kt * 1024 + mto * 128:kt * 1024 + (mto + 1) * 128],
                            hc[kt // 4][:, (kt % 4) * W:(kt % 4 + 1) * W],
                            start=(kt == 0), stop=(kt == 31))
                    fmm_td[(ci, mto)] = inc(g, "td", last)
                b.tensor(f_fmm)

                def f_fev(v, ci=ci, mto=mto):
                    v.wait_ge(sem["td"], fmm_td[(ci, mto)])
                    reg = g.psB[:, (mto % 2) * 512:(mto % 2) * 512 + W]
                    v.tensor_scalar(out=lns["lnT"][:, 0:W], in0=reg,
                                    scalar1=g.b2c[:, mto:mto + 1], scalar2=None,
                                    op0=ALU.add)
                    fev_vd[(ci, mto)] = inc(g, "vd", v.tensor_tensor(
                        out=tc[mto][:, :], in0=lns["lnT"][:, 0:W],
                        in1=x1c[mto][:, :], op=ALU.add))
                b.vector(f_fev)

            emit_ln(g, lns, [tc[pt][:, :] for pt in range(PT)],
                    [x1c[pt][:, :] for pt in range(PT)], W, pso=par * 512,
                    gamma=[g.g2c[:, pt:pt + 1] for pt in range(PT)],
                    beta=[g.bt2c[:, pt:pt + 1] for pt in range(PT)])
            vd_ln2 = cnt["vd"]

            def f_res(v, c0=c0, vd_ln2=vd_ln2, x1c=x1c):
                v.wait_ge(sem["vd"], vd_ln2)
                last = None
                for pt in range(PT):
                    last = v.tensor_tensor(out=g.xs[pt][:, c0:c0 + W],
                                           in0=g.xs[pt][:, c0:c0 + W],
                                           in1=x1c[pt][:, :], op=ALU.add)
                inc(g, "vd", last)
            b.vector(f_res)

        for ci in range(8):
            emit_front(ci)
            if ci > 0:
                emit_tail(ci - 1)
            emit_hstage(ci)
        emit_tail(7)
        join(g)


def emit_l1ffn(g):
    b, sem, cnt, nc = g.block, g.sem, g.cnt, g.nc
    g.tag += 1
    tg = g.tag
    with ExitStack() as ph:
        g.wT8 = ph.enter_context(nc.sbuf_tensor(f"ph{tg}_wT8", [8, 2048], BF16))
        sbc = lambda nm, sh, dt=F32: ph.enter_context(nc.sbuf_tensor(f"ph{tg}_{nm}", sh, dt))
        with ExitStack() as stA:
            t0 = [stA.enter_context(nc.sbuf_tensor(f"ph{tg}_t0_{i}", [128, T], BF16))
                  for i in range(PT)]
            lns = alloc_ln(g, stA, T, f"ph{tg}ln")
            emit_ln(g, lns, [g.xs[pt][:, :] for pt in range(PT)],
                    [t0[pt][:, :] for pt in range(PT)], T)
            emit_gating(g, t0, g.l1gw_sb, "l1")
            vd_g = cnt["vd"]

            def f_spill(q):
                q.wait_ge(sem["vd"], vd_g)
                for pt in range(PT):
                    inc(g, "ds", q.dma_start(out=g.t0_dram[pt * 128:(pt + 1) * 128, :],
                                             in_=t0[pt][:, :]), 16)
            b.sync(f_spill)
            ds_spill = cnt["ds"]
            join(g)

        wsl = [sbc(f"wsl{i}", [128, 11264], BF16) for i in range(4)]
        tch = [[sbc(f"tch{bi}_{i}", [128, 512], BF16) for i in range(PT)]
               for bi in range(2)]
        t0e = [sbc(f"t0e_{i}", [128, 512], BF16) for i in range(PT)]
        hshared = [sbc(f"hb_{i}", [128, 512], BF16) for i in range(2, 11)]
        hdup = [[sbc(f"hd{bi}_{i}", [128, 512], BF16) for i in range(2)]
                for bi in range(2)]

        def hbuf_of(idx, mc):
            return hdup[idx % 2][mc] if mc < 2 else hshared[mc - 2]
        scS = [sbc(f"scS{i}", [128, 512], BF16) for i in range(2)]

        roles = []
        cur = (0, 1, 2, 3)
        for e in range(8):
            roles.append(cur)
            cur = (cur[0], cur[3], cur[1], cur[2])

        WG, WU, WD = g.l1f_wg, g.l1f_wu, g.l1f_wd

        def slab_dma(q, dst, WSRC, e, nk):
            return q.dma_start(out=dst[:, :], in_=WSRC[e * 128:(e + 1) * 128, :])

        dgw = {}

        def f_w0(q):
            dgw[(0, "wg")] = inc(g, "dg", slab_dma(q, wsl[0], WG, 0, 8), 16)
            dgw[(0, "wu")] = inc(g, "dg", slab_dma(q, wsl[1], WU, 0, 8), 16)
            dgw[(0, "wd")] = inc(g, "dg", slab_dma(q, wsl[2], WD, 0, 11), 16)
        b.gpsimd(f_w0)

        def emit_wd(idx):
            e_, n_ = divmod(idx, 4)
            sd_w = roles[e_][2]
            for mt in range(PT):
                def f_wd(t, e=e_, n=n_, mt=mt, sd_=sd_w, idx=idx):
                    gi = idx * 8 + mt
                    slot = gi % 3
                    reg = g.psB[:, slot * 512:(slot + 1) * 512]
                    if mt == 0:
                        t.wait_ge(sem["dg"], dgw[(e, "wd")])
                        t.wait_ge(sem["vd"], hvd[(e, n, 10)])
                    if gi >= 3:
                        t.wait_ge(sem["vd"], evd[gi - 3])
                    last = None
                    for mc in range(11):
                        last = t.matmul(
                            reg,
                            wsl[sd_][:, mc * 1024 + mt * 128:mc * 1024 + (mt + 1) * 128],
                            hbuf_of(idx, mc)[:, :],
                            start=(mc == 0), stop=(mc == 10))
                    wdtd[(e, n, mt, "d")] = inc(g, "td", last)
                    if mt == 7:
                        wddone[(e, n)] = cnt["td"]
                b.tensor(f_wd)

                def f_ev(v, e=e_, n=n_, mt=mt, idx=idx):
                    gi = idx * 8 + mt
                    slot = gi % 3
                    v.wait_ge(sem["td"], wdtd[(e, n, mt, "d")])
                    evd[gi] = inc(g, "vd", v.scalar_tensor_tensor(
                        out=g.xs[mt][:, n * 512:(n + 1) * 512],
                        in0=g.psB[:, slot * 512:(slot + 1) * 512],
                        scalar=0.5,
                        in1=g.xs[mt][:, n * 512:(n + 1) * 512],
                        op0=ALU.mult, op1=ALU.add))
                b.vector(f_ev)

        gdone = {}
        udone = {}
        wddone = {}
        gchunk_done = {}
        bct = {}
        scale_vd = {}
        scale0 = {}
        hvd = {}
        sdsil = {}
        evd = {}
        wdtd = {}
        tchds = {}

        for e in range(8):
            sg, su, sd_, sfree = roles[e]

            for n in range(4):
                bi = (e * 4 + n) % 2

                def f_tch(q, e=e, n=n, bi=bi):
                    if e == 0 and n == 0:
                        q.wait_ge(sem["ds"], ds_spill)
                    idx = e * 4 + n
                    if idx >= 2:
                        pe, pn = divmod(idx - 2, 4)
                        q.wait_ge(sem["td"], gchunk_done[(pe, pn)])
                    for pt in range(PT):
                        tchds[(e, n)] = inc(g, "ds", q.dma_start(
                            out=tch[bi][pt][:, :],
                            in_=g.t0_dram[pt * 128:(pt + 1) * 128,
                                          n * 512:(n + 1) * 512]), 16)
                b.sync(f_tch)

                def f_bc(t, e=e, n=n):
                    if e == 0 and n == 0:
                        t.wait_ge(sem["sd"], g.sd_wT8)
                    if e > 0 or n > 0:
                        pe, pn = (e, n - 1) if n > 0 else (e - 1, 3)
                        t.wait_ge(sem["vd"], scale_vd[(pe, pn)])
                    bct[(e, n)] = inc(g, "td", t.matmul(
                        g.psB[:, 1536:2048],
                        g.oh8_bf[:, e * 128:(e + 1) * 128],
                        g.wT8[:, n * 512:(n + 1) * 512],
                        start=True, stop=True))
                b.tensor(f_bc)

                scale0[(e, n)] = cnt["vd"]

                def f_scale(v, e=e, n=n, bi=bi):
                    v.wait_ge(sem["td"], bct[(e, n)])
                    v.wait_ge(sem["ds"], tchds[(e, n)])
                    if e > 0 or n > 0:
                        pe, pn = (e, n - 1) if n > 0 else (e - 1, 3)
                        v.wait_ge(sem["td"], udone[(pe, pn)])
                    for kt in range(PT):
                        inc(g, "vd", v.tensor_tensor(out=t0e[kt][:, :],
                                                     in0=tch[bi][kt][:, :],
                                                     in1=g.psB[:, 1536:2048],
                                                     op=ALU.mult))
                    scale_vd[(e, n)] = cnt["vd"]
                b.vector(f_scale)

                idx_here = e * 4 + n
                if idx_here >= 1:
                    emit_wd(idx_here - 1)
                for mc in range(11):
                    def f_gmm(t, e=e, n=n, mc=mc, bi=bi, sg=sg):
                        if mc == 0:
                            t.wait_ge(sem["dg"], dgw[(e, "wg")])
                            t.wait_ge(sem["ds"], tchds[(e, n)])
                        idx = e * 4 + n
                        if mc >= 2:
                            t.wait_ge(sem["sd"], sdsil[(e, n, mc - 2)])
                        elif idx > 0:
                            pe_, pn_ = divmod(idx - 1, 4)
                            t.wait_ge(sem["sd"], sdsil[(pe_, pn_, 10 - mc)])
                        reg = g.psA[:, (mc % 2) * 512:(mc % 2) * 512 + 512]
                        last = None
                        for kt in range(PT):
                            last = t.matmul(
                                reg,
                                wsl[sg][:, kt * 1408 + mc * 128:kt * 1408 + (mc + 1) * 128],
                                tch[bi][kt][:, :],
                                start=(kt == 0), stop=(kt == 7))
                        wdtd[(e, n, mc, "g")] = inc(g, "td", last)
                        if mc == 10:
                            gchunk_done[(e, n)] = cnt["td"]
                            if n == 3:
                                gdone[e] = cnt["td"]
                    b.tensor(f_gmm)

                    def f_umm(t, e=e, n=n, mc=mc, su=su):
                        if mc == 0:
                            t.wait_ge(sem["dg"], dgw[(e, "wu")])
                        idx = e * 4 + n
                        if mc >= 2:
                            t.wait_ge(sem["vd"], hvd[(e, n, mc - 2)])
                        elif idx > 0:
                            pe_, pn_ = divmod(idx - 1, 4)
                            t.wait_ge(sem["vd"], hvd[(pe_, pn_, 10 - mc)])
                        reg = g.psA[:, 1024 + (mc % 2) * 512:1024 + (mc % 2) * 512 + 512]
                        last = None
                        for kt in range(PT):
                            if mc == 0:
                                t.wait_ge(sem["vd"], scale0[(e, n)] + kt + 1)
                            last = t.matmul(
                                reg,
                                wsl[su][:, kt * 1408 + mc * 128:kt * 1408 + (mc + 1) * 128],
                                t0e[kt][:, :],
                                start=(kt == 0), stop=(kt == 7))
                        wdtd[(e, n, mc, "u")] = inc(g, "td", last)
                        if mc == 10 and n == 3:
                            udone2 = cnt["td"]
                        if mc == 10:
                            udone[(e, n)] = cnt["td"]
                    b.tensor(f_umm)

                    def b_sil(s, e=e, n=n, mc=mc):
                        s.wait_ge(sem["td"], wdtd[(e, n, mc, "g")])
                        idx = e * 4 + n
                        if mc >= 2:
                            s.wait_ge(sem["vd"], hvd[(e, n, mc - 2)])
                        elif idx > 0:
                            pe_, pn_ = divmod(idx - 1, 4)
                            s.wait_ge(sem["vd"], hvd[(pe_, pn_, 10 - mc)])
                        reg = g.psA[:, (mc % 2) * 512:(mc % 2) * 512 + 512]
                        s.activation(out=scS[mc % 2][:, :], in_=reg,
                                     func=AF.Silu).then_inc(sem["sd"], 1)
                    sdsil[(e, n, mc)] = sdefer(g, 1, b_sil, 'silu')

                    def f_h(v, e=e, n=n, mc=mc):
                        v.wait_ge(sem["td"], wdtd[(e, n, mc, "u")])
                        v.wait_ge(sem["sd"], sdsil[(e, n, mc)])
                        idx = e * 4 + n
                        if idx >= 2 and mc < 2:
                            pe_, pn_ = divmod(idx - 2, 4)
                            v.wait_ge(sem["td"], wdtd[(pe_, pn_, 7, "d")])
                        reg = g.psA[:, 1024 + (mc % 2) * 512:1024 + (mc % 2) * 512 + 512]
                        idx = e * 4 + n
                        hvd[(e, n, mc)] = inc(g, "vd", v.tensor_tensor(
                            out=hbuf_of(idx, mc)[:, :], in0=reg,
                            in1=scS[mc % 2][:, :], op=ALU.mult))
                    b.vector(f_h)


            if e == 7:
                emit_wd(31)
            if e < 7:
                def f_pref(q, e=e):
                    nsg, nsu, nsd, nfree = roles[e + 1]
                    if e >= 1:
                        q.wait_ge(sem["td"], wddone[(e - 1, 3)])
                    dgw[(e + 1, "wu")] = inc(g, "dg", slab_dma(q, wsl[nsu], WU, e + 1, 8), 16)
                    q.wait_ge(sem["td"], gdone[e])
                    dgw[(e + 1, "wg")] = inc(g, "dg", slab_dma(q, wsl[nsg], WG, e + 1, 8), 16)
                    q.wait_ge(sem["td"], udone[(e, 3)])
                    dgw[(e + 1, "wd")] = inc(g, "dg", slab_dma(q, wsl[nsd], WD, e + 1, 11), 16)
                b.gpsimd(f_pref)
        join(g)



def emit_l1attn(g):
    b, sem, cnt, nc = g.block, g.sem, g.cnt, g.nc
    g.tag += 1
    tg = g.tag
    with ExitStack() as ph:
        sbc = lambda nm, sh, dt=F32: ph.enter_context(nc.sbuf_tensor(f"ph{tg}_{nm}", sh, dt))
        tt0 = [sbc(f"t_{i}", [128, T], BF16) for i in range(PT)]
        pk = sbc("pk", [128, 24])
        halo = sbc("halo", [128, 24])
        packS = sbc("packS", [128, 16])
        gath = sbc("gath", [128, 128])
        carryc = sbc("carryc", [128, 8])
        cfold = sbc("cfold", [128, 8])
        colt = sbc("colt", [128, 1])

        with ExitStack() as stA:
            lns = alloc_ln(g, stA, T, f"ph{tg}ln")
            emit_ln(g, lns, [g.xs[pt][:, :] for pt in range(PT)],
                    [tt0[pt][:, :] for pt in range(PT)], T)
            join(g)
        x1bf = [sbc(f"x1_{i}", [128, 2056], BF16) for i in range(PT)]

        stBC = ExitStack()
        gi_sl = stBC.enter_context(nc.sbuf_tensor(f"ph{tg}_gisl", [128, 8192], BF16))
        ga_sl = stBC.enter_context(nc.sbuf_tensor(f"ph{tg}_gasl", [128, 8192], BF16))
        with ExitStack() as stB:
            wx = stB.enter_context(nc.sbuf_tensor(f"ph{tg}_wx", [128, 8192], BF16))

            def f_wx(q):
                inc(g, "dg", q.dma_start(out=wx[:, :], in_=g.l1a_wx), 16)
                for dst, W_ in ((gi_sl, g.l1a_gi), (ga_sl, g.l1a_ga)):
                    inc(g, "dg", q.dma_start(out=dst[:, :], in_=W_), 16)
            b.gpsimd(f_wx)
            dg_wx = cnt["dg"]

            x1td = {}
            x1vd = {}
            for ni, n in enumerate([3, 0, 1, 2]):
                for mt in range(PT):
                    def f_mm(t, n=n, mt=mt, ni=ni):
                        gi = ni * 8 + mt
                        if gi == 0:
                            t.wait_ge(sem["dg"], dg_wx)
                        slot = gi % 4
                        reg = g.psB[:, slot * 512:(slot + 1) * 512]
                        if gi >= 4:
                            t.wait_ge(sem["vd"], x1vd[gi - 4])
                        last = None
                        for kt in range(PT):
                            last = t.matmul(
                                reg, wx[:, kt * 1024 + mt * 128:kt * 1024 + (mt + 1) * 128],
                                tt0[kt][:, n * 512:(n + 1) * 512],
                                start=(kt == 0), stop=(kt == 7))
                        x1td[gi] = inc(g, "td", last)
                    b.tensor(f_mm)

                    def f_ev(v, n=n, mt=mt, ni=ni):
                        gi = ni * 8 + mt
                        slot = gi % 4
                        v.wait_ge(sem["td"], x1td[gi])
                        x1vd[gi] = inc(g, "vd", v.tensor_scalar(
                            out=x1bf[mt][:, 3 + n * 512:3 + (n + 1) * 512],
                            in0=g.psB[:, slot * 512:(slot + 1) * 512],
                            scalar1=g.bxc[:, mt:mt + 1], scalar2=None,
                            op0=ALU.add))
                    b.vector(f_ev)
                if ni == 0:
                    def f_pub(v):
                        last = None
                        for pt in range(PT):
                            last = v.tensor_copy(out=pk[:, pt * 3:(pt + 1) * 3],
                                                 in_=x1bf[pt][:, 2048:2051])
                        inc(g, "vd", last)
                    b.vector(f_pub)
                    vd_pub = cnt["vd"]

                    def f_cc(q):
                        q.wait_ge(sem["vd"], vd_pub)
                        inc(g, "dg", q.dma_start(out=g.cc1_in[:, :], in_=pk[:, :]), 16)
                        q.wait_ge(sem["dg"], cnt["dg"])
                        inc(g, "cc", q.collective_compute(
                            "AllGather", ALU.bypass,
                            replica_groups=[list(range(NCORES))],
                            ins=[g.cc1_in.ap().opt()],
                            outs=[g.cc1_out.ap().opt()]))
                    b.gpsimd(f_cc)
                    cc_1 = cnt["cc"]

                    def f_halo(q):
                        with q.register("rh") as rh:
                            q.wait_ge(sem["cc"], cc_1)
                            q.reg_load(rh, g.prev_in)
                            q.reg_mul(rh, rh, 24 * 128)
                            inc(g, "ds", q.dma_start(
                                out=halo[:, :],
                                in_=bass.AP(g.cc1_out, rh, [[24, 128], [1, 24]])), 16)
                    b.sync(f_halo)
                    ds_h = cnt["ds"]

                    def f_mask(v):
                        def settle(ins):
                            v.wait_ge(sem["vd"], inc(g, "vd", ins))
                        v.wait_ge(sem["ds"], ds_h)
                        settle(v.tensor_scalar(out=halo[:, :], in0=halo[:, :],
                                               scalar1=g.mcol[:, :],
                                               scalar2=None, op0=ALU.mult))
                        for pt in range(PT):
                            settle(v.tensor_copy(out=x1bf[pt][:, 0:3],
                                                 in_=halo[:, pt * 3:(pt + 1) * 3]))
                    b.vector(f_mask)
            dg_halo = cnt["dg"]
            vd_halo = cnt["vd"]
            join(g)

        with ExitStack() as stC:
            sc2 = lambda nm, sh, dt=F32: stC.enter_context(nc.sbuf_tensor(f"ph{tg}_{nm}", sh, dt))
            x2b = [[sc2(f"x2b{bi}_{i}", [128, 256], BF16) for i in range(PT)]
                   for bi in range(2)]
            x2f1 = sc2("x2f1", [128, 256])
            x2f = [x2f1 for _ in range(PT)]
            gx2 = [sc2(f"gx{i}", [128, 256], BF16) for i in range(2)]
            sgF = [sc2(f"sg{i}", [128, 256], BF16) for i in range(2)]
            mlt = [sc2(f"ml{i}", [128, 256], BF16) for i in range(2)]
            lac = [sc2(f"lac{i}", [128, 256], BF16) for i in range(2)]
            bc2 = [sc2(f"bc2{i}", [128, 256], BF16) for i in range(2)]

            dg_g = cnt["dg"]

            cvd = {}
            gtd = {}
            atd = {}
            gsd = {}
            asd = {}
            bvd = {}
            lads = {}
            chunk_order = [1, 2, 3, 4, 5, 6, 7, 0]
            pos_of = {ci: p for p, ci in enumerate(chunk_order)}

            def emit_conv(ci):
                c0 = ci * 256

                def f_conv(v, ci=ci, c0=c0):
                    pos = pos_of[ci]
                    if pos == 0:
                        v.wait_ge(sem["vd"], cnt["vd"])
                    if ci == 0:
                        v.wait_ge(sem["vd"], vd_halo)
                    if pos >= 2:
                        v.wait_ge(sem["td"], atd[(pos - 2) * 8 + 7])
                    last = None
                    for pt in range(PT):
                        v.tensor_scalar(out=x2f[pt][:, :],
                                        in0=x1bf[pt][:, c0:c0 + 256],
                                        scalar1=g.cwc[:, pt:pt + 1],
                                        scalar2=g.cbc[:, pt:pt + 1],
                                        op0=ALU.mult, op1=ALU.add)
                        for i in (1, 2):
                            v.scalar_tensor_tensor(
                                out=x2f[pt][:, :],
                                in0=x1bf[pt][:, c0 + i:c0 + i + 256],
                                scalar=g.cwc[:, i * 8 + pt:i * 8 + pt + 1],
                                in1=x2f[pt][:, :], op0=ALU.mult, op1=ALU.add)
                        last = v.scalar_tensor_tensor(
                            out=x2b[pos_of[ci] % 2][pt][:, :],
                            in0=x1bf[pt][:, c0 + 3:c0 + 3 + 256],
                            scalar=g.cwc[:, 24 + pt:24 + pt + 1],
                            in1=x2f[pt][:, :], op0=ALU.mult, op1=ALU.add)
                    cvd[ci] = inc(g, "vd", last)
                b.vector(f_conv)

            emit_conv(chunk_order[0])
            for pos, ci in enumerate(chunk_order):
                c0 = ci * 256
                if pos + 1 < 8:
                    emit_conv(chunk_order[pos + 1])
                for mt in range(PT):
                    def f_gmm(t, ci=ci, mt=mt, c0=c0, pos=pos):
                        gi = pos * 8 + mt
                        if gi == 0:
                            t.wait_ge(sem["dg"], dg_g)
                        if mt == 0:
                            t.wait_ge(sem["vd"], cvd[ci])
                        if gi >= 2:
                            t.wait_ge(sem["sd"], gsd[(gi - 2, "gx")])
                        reg = g.psA[:, (gi % 2) * 512:(gi % 2) * 512 + 256]
                        last = None
                        for kt in range(PT):
                            last = t.matmul(
                                reg, gi_sl[:, kt * 1024 + mt * 128:kt * 1024 + (mt + 1) * 128],
                                x2b[pos % 2][kt][:, :], start=(kt == 0), stop=(kt == 7))
                        gtd[gi] = inc(g, "td", last)
                    b.tensor(f_gmm)

                    def b_gev(s, ci=ci, mt=mt, pos=pos):
                        gi = pos * 8 + mt
                        s.wait_ge(sem["td"], gtd[gi])
                        if gi >= 2:
                            s.wait_ge(sem["vd"], bvd.get(gi - 2, 0))
                        reg = g.psA[:, (gi % 2) * 512:(gi % 2) * 512 + 256]
                        s.activation(
                            out=gx2[gi % 2][:, :], in_=reg, func=AF.Sigmoid,
                            bias=g.gibc[:, mt:mt + 1]).then_inc(sem["sd"], 1)
                    gsd[(pos * 8 + mt, "gx")] = sdefer(g, 1, b_gev, 'sig')

                    def f_amm(t, ci=ci, mt=mt, c0=c0, pos=pos):
                        gi = pos * 8 + mt
                        if gi >= 2:
                            t.wait_ge(sem["sd"], asd[(gi - 2, "la")])
                        reg = g.psA[:, 1024 + (gi % 2) * 512:1024 + (gi % 2) * 512 + 256]
                        last = None
                        for kt in range(PT):
                            last = t.matmul(
                                reg, ga_sl[:, kt * 1024 + mt * 128:kt * 1024 + (mt + 1) * 128],
                                x2b[pos % 2][kt][:, :], start=(kt == 0), stop=(kt == 7))
                        atd[gi] = inc(g, "td", last)
                    b.tensor(f_amm)

                    def b_aev(s, ci=ci, mt=mt, pos=pos):
                        gi = pos * 8 + mt
                        s.wait_ge(sem["td"], atd[gi])
                        if gi >= 2:
                            s.wait_ge(sem["vd"], bvd[(gi - 2, "lac")])
                        reg = g.psA[:, 1024 + (gi % 2) * 512:1024 + (gi % 2) * 512 + 256]
                        s.activation(
                            out=sgF[gi % 2][:, :], in_=reg, func=AF.Sigmoid,
                            bias=g.gabc[:, mt:mt + 1]).then_inc(sem["sd"], 1)
                    asd[(pos * 8 + mt, "sg")] = sdefer(g, 1, b_aev, 'sig')

                    def f_la(v, ci=ci, mt=mt, pos=pos):
                        gi = pos * 8 + mt
                        v.wait_ge(sem["sd"], asd[(gi, "sg")])
                        if gi >= 2:
                            v.wait_ge(sem["ds"], lads[gi - 2])
                            v.wait_ge(sem["sd"], asd[(gi - 2, "la")])
                        bvd[(gi, "lac")] = inc(g, "vd", v.tensor_scalar(
                            out=lac[gi % 2][:, :], in0=sgF[gi % 2][:, :],
                            scalar1=g.m8sp[:, mt:mt + 1], scalar2=None,
                            op0=ALU.mult))
                    b.vector(f_la)

                    def f_ladma(q, ci=ci, mt=mt, c0=c0, pos=pos):
                        gi = pos * 8 + mt
                        q.wait_ge(sem["vd"], bvd[(gi, "lac")])
                        lads[gi] = inc(g, "ds", q.dma_start(
                            out=g.la_dram[mt * 128:(mt + 1) * 128, c0:c0 + 256],
                            in_=lac[gi % 2][:, :]), 16)
                    b.sync(f_ladma)

                    def b_mlt(s, ci=ci, mt=mt, pos=pos):
                        gi = pos * 8 + mt
                        s.wait_ge(sem["vd"], bvd[(gi, "lac")])
                        if gi >= 2:
                            s.wait_ge(sem["vd"], bvd[gi - 2])
                        s.activation(
                            out=mlt[gi % 2][:, :], in_=lac[gi % 2][:, :],
                            func=AF.Exp, scale=2.0).then_inc(sem["sd"], 1)
                    asd[(pos * 8 + mt, "la")] = sdefer(g, 1, b_mlt, 'exp')

                    def f_mlt2(v, ci=ci, mt=mt, pos=pos):
                        gi = pos * 8 + mt
                        v.wait_ge(sem["sd"], asd[(gi, "la")])
                        v.tensor_scalar(out=mlt[gi % 2][:, :], in0=mlt[gi % 2][:, :],
                                        scalar1=-1.0, scalar2=1.0,
                                        op0=ALU.mult, op1=ALU.add)
                        inc(g, "vd", v.tensor_scalar(out=mlt[gi % 2][:, :],
                                                     in0=mlt[gi % 2][:, :],
                                                     scalar1=0.0, scalar2=None,
                                                     op0=ALU.max))
                    b.vector(f_mlt2)
                    vd_m2 = cnt["vd"]

                    def b_mlt3(s, ci=ci, mt=mt, vd_m2=vd_m2, pos=pos):
                        gi = pos * 8 + mt
                        s.wait_ge(sem["vd"], vd_m2)
                        s.activation(
                            out=mlt[gi % 2][:, :], in_=mlt[gi % 2][:, :],
                            func=AF.Sqrt).then_inc(sem["sd"], 1)
                    asd[(pos * 8 + mt, "sqrt")] = sdefer(g, 1, b_mlt3, 'sqrt')

                    def f_b(v, ci=ci, mt=mt, c0=c0, pos=pos):
                        gi = pos * 8 + mt
                        v.wait_ge(sem["sd"], asd[(gi, "sqrt")])
                        v.wait_ge(sem["sd"], gsd[(gi, "gx")])
                        if ci == 0:
                            v.wait_ge(sem["vd"], inc(g, "vd", v.tensor_scalar(
                                out=colt[:, :], in0=mlt[gi % 2][:, 0:1],
                                scalar1=g.mcol[:, :], scalar2=None, op0=ALU.mult)))
                            v.wait_ge(sem["vd"], inc(g, "vd", v.tensor_tensor(
                                out=mlt[gi % 2][:, 0:1], in0=colt[:, :],
                                in1=g.m1col[:, :], op=ALU.add)))
                        if gi >= 2:
                            v.wait_ge(sem["ds"], lads[(gi - 2, "b")])
                        v.tensor_tensor(out=bc2[gi % 2][:, :], in0=gx2[gi % 2][:, :],
                                        in1=x2b[pos % 2][mt][:, :], op=ALU.mult)
                        last = v.tensor_tensor(out=bc2[gi % 2][:, :],
                                               in0=bc2[gi % 2][:, :],
                                               in1=mlt[gi % 2][:, :], op=ALU.mult)
                        bvd[gi] = inc(g, "vd", last)
                    b.vector(f_b)

                    def f_bdma(q, ci=ci, mt=mt, c0=c0, pos=pos):
                        gi = pos * 8 + mt
                        q.wait_ge(sem["vd"], bvd[gi])
                        lads[(gi, "b")] = inc(g, "ds", q.dma_start(
                            out=g.b_dram[mt * 128:(mt + 1) * 128, c0:c0 + 256],
                            in_=bc2[gi % 2][:, :]), 16)
                    b.sync(f_bdma)
            join(g)
        stBC.close()

        with ExitStack() as stD:
            wy = stD.enter_context(nc.sbuf_tensor(f"ph{tg}_wy", [128, 8192], BF16))

            def f_wy(q):
                inc(g, "dg", q.dma_start(out=wy[:, :], in_=g.l1a_wy), 16)
            b.gpsimd(f_wy)
            dg_wy = cnt["dg"]
            larow = stD.enter_context(nc.sbuf_tensor(f"ph{tg}_lar", [128, T], BF16))
            brow = stD.enter_context(nc.sbuf_tensor(f"ph{tg}_brw", [128, T], BF16))
            af32 = stD.enter_context(nc.sbuf_tensor(f"ph{tg}_af", [128, T], F32))

            yfull = [stD.enter_context(nc.sbuf_tensor(f"ph{tg}_yf{i}", [128, T], BF16))
                     for i in range(PT)]
            hyb = stD.enter_context(nc.sbuf_tensor(f"ph{tg}_hy", [128, 512], BF16))
            ytd = {}
            ysd = {}
            yvd = {}
            scan_vd = {}
            lrds = {}
            for p_ass in (1, 2):
                for pt in range(PT):
                    prev = None
                    if (p_ass, pt) != (1, 0):
                        prev = (p_ass, pt - 1) if pt > 0 else (p_ass - 1, 7)

                    def f_lrd(q, pt=pt, p_ass=p_ass, prev=prev):
                        if prev is not None:
                            q.wait_ge(sem["sd"], asd[("exp",) + prev])
                            q.wait_ge(sem["vd"], scan_vd[prev])
                        inc(g, "ds", q.dma_start(
                            out=larow[:, :],
                            in_=g.la_dram[pt * 128:(pt + 1) * 128, :]), 16)
                        lrds[(p_ass, pt)] = inc(g, "ds", q.dma_start(
                            out=brow[:, :],
                            in_=g.b_dram[pt * 128:(pt + 1) * 128, :]), 16)
                    b.sync(f_lrd)

                    def b_exp(s, pt=pt, p_ass=p_ass, prev=prev):
                        s.wait_ge(sem["ds"], lrds[(p_ass, pt)])
                        if prev is not None:
                            s.wait_ge(sem["vd"], scan_vd[prev])
                        s.activation(out=af32[:, :], in_=larow[:, :],
                                     func=AF.Exp).then_inc(sem["sd"], 1)
                    asd[("exp", p_ass, pt)] = sdefer(g, 1, b_exp, 'exp')

                    def f_patch(v, pt=pt, p_ass=p_ass):
                        v.wait_ge(sem["sd"], asd[("exp", p_ass, pt)])
                        inc(g, "vd", v.tensor_scalar(
                            out=af32[:, 0:1], in0=af32[:, 0:1],
                            scalar1=g.mcol[:, :], scalar2=None, op0=ALU.mult))
                    b.vector(f_patch)
                    vd_p = cnt["vd"]

                    if p_ass == 1:
                        def f_scan1(v, pt=pt, vd_p=vd_p):
                            v.wait_ge(sem["vd"], vd_p)
                            v.tensor_tensor_scan(out=x1bf[pt][:, 0:T],
                                                 data0=af32[:, :],
                                                 data1=brow[:, :],
                                                 initial=0.0,
                                                 op0=ALU.mult, op1=ALU.add)
                            v.wait_ge(sem["vd"], inc(g, "vd", v.tensor_reduce(
                                out=colt[:, :], in_=larow[:, :], axis=AX,
                                op=ALU.add)))
                            scan_vd[(1, pt)] = inc(g, "vd", v.tensor_copy(
                                out=packS[:, 2 * pt + 1:2 * pt + 2],
                                in_=x1bf[pt][:, T - 1:T]))
                        b.vector(f_scan1)
                        svd = cnt["vd"]

                        def b_Ak(s, pt=pt, svd=svd):
                            s.wait_ge(sem["vd"], svd)
                            s.activation(
                                out=packS[:, 2 * pt:2 * pt + 1], in_=colt[:, :],
                                func=AF.Exp).then_inc(sem["sd"], 1)
                        asd[("Ak", pt)] = sdefer(g, 1, b_Ak, 'exp')

                        for n in range(4):
                            yi_ = pt * 4 + n
                            def f_ymm(t, n=n, pt=pt):
                                yi = pt * 4 + n
                                if yi == 0:
                                    t.wait_ge(sem["dg"], dg_wy)
                                    t.wait_ge(sem["vd"], cnt["vd"])
                                if yi >= 2:
                                    t.wait_ge(sem["sd"], ysd[yi - 2])
                                reg = g.psA[:, (yi % 2) * 512:(yi % 2) * 512 + 512]
                                last = None
                                for kt in range(PT):
                                    last = t.matmul(
                                        reg, wy[:, kt * 1024 + pt * 128:kt * 1024 + (pt + 1) * 128],
                                        tt0[kt][:, n * 512:(n + 1) * 512],
                                        start=(kt == 0), stop=(kt == 7))
                                ytd[yi] = inc(g, "td", last)
                            b.tensor(f_ymm)

                            def b_yev(s, n=n, pt=pt):
                                yi = pt * 4 + n
                                s.wait_ge(sem["td"], ytd[yi])
                                reg = g.psA[:, (yi % 2) * 512:(yi % 2) * 512 + 512]
                                s.activation(
                                    out=yfull[pt][:, n * 512:(n + 1) * 512],
                                    in_=reg, func=AF.Gelu,
                                    bias=g.byc[:, pt:pt + 1]).then_inc(sem["sd"], 1)
                            ysd[yi_] = sdefer(g, 1, b_yev, 'gelu')
                    else:
                        def f_scan2(v, pt=pt, vd_p=vd_p):
                            v.wait_ge(sem["vd"], vd_p)
                            scan_vd[(2, pt)] = inc(g, "vd", v.tensor_tensor_scan(
                                out=x1bf[pt][:, 0:T], data0=af32[:, :],
                                data1=brow[:, :],
                                initial=carryc[:, pt:pt + 1],
                                op0=ALU.mult, op1=ALU.add))
                        b.vector(f_scan2)

                        for n in range(4):
                            def f_out(v, n=n, pt=pt):
                                yi = pt * 4 + n
                                v.wait_ge(sem["sd"], ysd[yi])
                                v.wait_ge(sem["vd"], scan_vd[(2, pt)])
                                v.tensor_tensor(out=hyb[:, :],
                                                in0=x1bf[pt][:, n * 512:(n + 1) * 512],
                                                in1=yfull[pt][:, n * 512:(n + 1) * 512],
                                                op=ALU.mult)
                                yvd[yi] = inc(g, "vd", v.tensor_tensor(
                                    out=g.xs[pt][:, n * 512:(n + 1) * 512],
                                    in0=g.xs[pt][:, n * 512:(n + 1) * 512],
                                    in1=hyb[:, :], op=ALU.add))
                            b.vector(f_out)

                if p_ass == 1:
                    sd_all = cnt["sd"]

                    def f_cc2(q):
                        q.wait_ge(sem["sd"], sd_all)
                        q.wait_ge(sem["vd"], cnt["vd"])
                        inc(g, "dg", q.dma_start(out=g.cc2_in[:, :], in_=packS[:, :]), 16)
                        q.wait_ge(sem["dg"], cnt["dg"])
                        inc(g, "cc", q.collective_compute(
                            "AllGather", ALU.bypass,
                            replica_groups=[list(range(NCORES))],
                            ins=[g.cc2_in.ap().opt()],
                            outs=[g.cc2_out.ap().opt()]))
                        q.wait_ge(sem["cc"], cnt["cc"])
                        for r in range(NCORES):
                            inc(g, "dg", q.dma_start(
                                out=gath[:, r * 16:(r + 1) * 16],
                                in_=g.cc2_out[r * 128:(r + 1) * 128, :]), 16)
                    b.gpsimd(f_cc2)
                    dg_cc2 = cnt["dg"]

                    def f_fold(v):
                        def settle(ins):
                            v.wait_ge(sem["vd"], inc(g, "vd", ins))
                        v.wait_ge(sem["dg"], dg_cc2)
                        for pt in range(PT):
                            settle(v.tensor_tensor_scan(
                                out=cfold[:, :],
                                data0=gath[:, 2 * pt::16],
                                data1=gath[:, 2 * pt + 1::16],
                                initial=0.0, op0=ALU.mult, op1=ALU.add))
                            settle(v.tensor_tensor(out=cfold[:, :], in0=cfold[:, :],
                                                   in1=g.onehot[:, :], op=ALU.mult))
                            settle(v.tensor_reduce(out=carryc[:, pt:pt + 1],
                                                   in_=cfold[:, :], axis=AX,
                                                   op=ALU.add))
                    b.vector(f_fold)
            join(g)

        join(g)


def build_program(phases):
    nc = bass.Bass("TRN2", target_bir_lowering=False)
    g = G()
    g.nc = nc
    g.tag = 0
    g.sc_pend = []

    ap = lambda t: t.ap()
    g.x_in = ap(nc.dram_tensor("x_in", [D, T], F32, kind="ExternalInput"))
    g.x_out = ap(nc.dram_tensor("x_out", [D, T], F32, kind="ExternalOutput"))
    g.l0f_w = ap(nc.dram_tensor("l0f_w", [8 * 128, 8192], F32, kind="ExternalInput"))
    g.l0f_b = ap(nc.dram_tensor("l0f_b", [8, 1024], F32, kind="ExternalInput"))
    g.l0f_gw = ap(nc.dram_tensor("l0f_gw", [128, 64], F32, kind="ExternalInput"))
    g.l0f_gb = ap(nc.dram_tensor("l0f_gb", [1, 8], F32, kind="ExternalInput"))
    g.eye_in = ap(nc.dram_tensor("eye", [128, 128], F32, kind="ExternalInput"))
    g.oh8_in = ap(nc.dram_tensor("oh8", [8, 1024], F32, kind="ExternalInput"))
    g.mcol_in = ap(nc.dram_tensor("mcol", [128, 1], F32, kind="ExternalInput"))
    g.oneh_in = ap(nc.dram_tensor("onehot", [128, 8], F32, kind="ExternalInput"))
    g.prev_in = ap(nc.dram_tensor("prev_idx", [1, 1], U32, kind="ExternalInput"))
    if DEBUG_TAPS:
        g.tap_t0 = ap(nc.dram_tensor("tap_t0", [D, T], BF16, kind="ExternalOutput"))
        g.tap_gw = ap(nc.dram_tensor("tap_gw", [128, 128], F32, kind="ExternalOutput"))
        g.tap_wT8 = ap(nc.dram_tensor("tap_wT8", [8, 2048], BF16, kind="ExternalOutput"))
        g.tap_lg = ap(nc.dram_tensor("tap_lg", [128, 128], F32, kind="ExternalOutput"))
        g.tap_nm = ap(nc.dram_tensor("tap_nm", [128, 16], F32, kind="ExternalOutput"))
        g.tap_ex = ap(nc.dram_tensor("tap_ex", [128, 128], F32, kind="ExternalOutput"))
    if phases >= 2:
        g.l0a_w1t = ap(nc.dram_tensor("l0a_w1t", [128, 2048], F32, kind="ExternalInput"))
        g.l0a_w2t = ap(nc.dram_tensor("l0a_w2t", [128, 32768], F32, kind="ExternalInput"))
        g.l0a_b1 = ap(nc.dram_tensor("l0a_b1", [128, 32], F32, kind="ExternalInput"))
        g.l0a_b2 = ap(nc.dram_tensor("l0a_b2", [128, 8], F32, kind="ExternalInput"))
        g.l0a_g1 = ap(nc.dram_tensor("l0a_g1", [128, 8], F32, kind="ExternalInput"))
        g.l0a_bt1 = ap(nc.dram_tensor("l0a_bt1", [128, 8], F32, kind="ExternalInput"))
        g.l0a_g2 = ap(nc.dram_tensor("l0a_g2", [128, 8], F32, kind="ExternalInput"))
        g.l0a_bt2 = ap(nc.dram_tensor("l0a_bt2", [128, 8], F32, kind="ExternalInput"))
    if phases >= 4:
        g.l1f_gw = ap(nc.dram_tensor("l1f_gw", [128, 64], F32, kind="ExternalInput"))
        g.l1f_wg = ap(nc.dram_tensor("l1f_wg", [8 * 128, 11264], F32, kind="ExternalInput"))
        g.l1f_wu = ap(nc.dram_tensor("l1f_wu", [8 * 128, 11264], F32, kind="ExternalInput"))
        g.l1f_wd = ap(nc.dram_tensor("l1f_wd", [8 * 128, 11264], F32, kind="ExternalInput"))
        g.t0_dram = ap(nc.dram_tensor("t0_dram", [D, T], BF16))
    if phases >= 5:
        g.l1a_wx = ap(nc.dram_tensor("l1a_wx", [128, 8192], F32, kind="ExternalInput"))
        g.l1a_wy = ap(nc.dram_tensor("l1a_wy", [128, 8192], F32, kind="ExternalInput"))
        g.l1a_gi = ap(nc.dram_tensor("l1a_gi", [128, 8192], F32, kind="ExternalInput"))
        g.l1a_ga = ap(nc.dram_tensor("l1a_ga", [128, 8192], F32, kind="ExternalInput"))
        g.l1a_bx = ap(nc.dram_tensor("l1a_bx", [128, 8], F32, kind="ExternalInput"))
        g.l1a_by = ap(nc.dram_tensor("l1a_by", [128, 8], F32, kind="ExternalInput"))
        g.l1a_gib = ap(nc.dram_tensor("l1a_gib", [128, 8], F32, kind="ExternalInput"))
        g.l1a_gab = ap(nc.dram_tensor("l1a_gab", [128, 8], F32, kind="ExternalInput"))
        g.l1a_cb = ap(nc.dram_tensor("l1a_cb", [128, 8], F32, kind="ExternalInput"))
        g.l1a_cw = ap(nc.dram_tensor("l1a_cw", [128, 32], F32, kind="ExternalInput"))
        g.l1a_ap = ap(nc.dram_tensor("l1a_ap", [128, 8], F32, kind="ExternalInput"))
        g.t_dram = ap(nc.dram_tensor("t_dram", [D, T], BF16))
        g.la_dram = ap(nc.dram_tensor("la_dram", [D, T], BF16))
        g.b_dram = ap(nc.dram_tensor("b_dram", [D, T], BF16))
        g.cc1_in = nc.dram_tensor("cc1_in", [128, 24], F32)
        g.cc1_out = nc.dram_tensor("cc1_out", [1024, 24], F32)
        g.cc2_in = nc.dram_tensor("cc2_in", [128, 16], F32)
        g.cc2_out = nc.dram_tensor("cc2_out", [1024, 16], F32)

    with ExitStack() as stack:
        g.sem = {}
        g.cnt = {}
        for name in ("ds", "dg", "td", "vd", "sd", "cc"):
            g.sem[name] = stack.enter_context(nc.semaphore(name))
            g.cnt[name] = 0
        sb = lambda nm, sh, dt=F32: stack.enter_context(nc.sbuf_tensor(nm, sh, dt))
        g.xs = [sb(f"x{i}", [128, T]) for i in range(PT)]
        g.psA = stack.enter_context(nc.psum_tensor("psA", [128, 2048], F32))
        g.psB = stack.enter_context(nc.psum_tensor("psB", [128, 2048], F32))
        g.eye32 = sb("eye32", [128, 128])
        g.ones_bf = sb("ones_bf", [128, 128], BF16)
        g.ones1_f32 = sb("ones1_f32", [1, 128])
        g.zero64 = sb("zero64", [128, 64])
        g.halfpi = sb("halfpi", [128, 1])
        g.mcol = sb("mcol_sb", [128, 1])
        g.m1col = sb("m1col_sb", [128, 1])
        g.onehot = sb("onehot_sb", [128, 8])
        g.gatw = sb("gatw", [128, 128])
        g.gsc = sb("gsc", [128, 128])
        g.gs2 = sb("gs2", [128, 128])
        g.gis1 = sb("gis1", [128, 128])
        g.gis2 = sb("gis2", [128, 128])
        g.gnm = sb("gnm", [128, 16])
        g.colA = sb("colA", [128, 1]); g.colB = sb("colB", [128, 1])
        g.colC = sb("colC", [128, 16]); g.colP = sb("colP", [128, 16])

        g.l0gw_sb = sb("l0gw_sb", [128, 64], BF16)
        g.oh8_bf = sb("oh8_bf", [8, 1024], BF16)
        g.gbrow_bf = sb("gbrow_bf", [1, 8], BF16)
        g.l0fb_bf = sb("l0fb_bf", [8, 1024], BF16)
        if phases >= 2:
            g.b1c = sb("b1c", [128, 32]); g.b2c = sb("b2c", [128, 8])
            g.g1c = sb("g1c", [128, 8]); g.bt1c = sb("bt1c", [128, 8])
            g.g2c = sb("g2c", [128, 8]); g.bt2c = sb("bt2c", [128, 8])
        if phases >= 4:
            g.l1gw_sb = sb("l1gw_sb", [128, 64], BF16)
        if phases >= 5:
            g.bxc = sb("bxc", [128, 8]); g.byc = sb("byc", [128, 8])
            g.gibc = sb("gibc", [128, 8]); g.gabc = sb("gabc", [128, 8])
            g.cbc = sb("cbc", [128, 8]); g.cwc = sb("cwc", [128, 32])
            g.apc = sb("apc", [128, 8]); g.m8sp = sb("m8sp", [128, 8])

        b = g.block = nc.Block()
        with b:
            def f_init_dma(q):
                for pt in range(PT):
                    inc(g, "ds", q.dma_start(out=g.xs[pt][:, :],
                                             in_=g.x_in[pt * 128:(pt + 1) * 128, :]), 16)
                inc(g, "ds", q.dma_start(out=g.eye32[:, :], in_=g.eye_in), 16)
                inc(g, "ds", q.dma_start(out=g.mcol[:, :], in_=g.mcol_in), 16)
                inc(g, "ds", q.dma_start(out=g.onehot[:, :], in_=g.oneh_in), 16)
                if phases >= 2:
                    for dst, src in [(g.b1c, g.l0a_b1), (g.b2c, g.l0a_b2),
                                     (g.g1c, g.l0a_g1), (g.bt1c, g.l0a_bt1),
                                     (g.g2c, g.l0a_g2), (g.bt2c, g.l0a_bt2)]:
                        inc(g, "ds", q.dma_start(out=dst[:, :], in_=src), 16)
                if phases >= 5:
                    for dst, src in [(g.bxc, g.l1a_bx), (g.byc, g.l1a_by),
                                     (g.gibc, g.l1a_gib), (g.gabc, g.l1a_gab),
                                     (g.cbc, g.l1a_cb), (g.cwc, g.l1a_cw),
                                     (g.apc, g.l1a_ap)]:
                        inc(g, "ds", q.dma_start(out=dst[:, :], in_=src), 16)
            b.sync(f_init_dma)

            def f_init_gdma(q):
                inc(g, "dg", q.dma_start(out=g.l0gw_sb[:, :], in_=g.l0f_gw), 16)
                inc(g, "dg", q.dma_start(out=g.oh8_bf[:, :], in_=g.oh8_in), 16)
                inc(g, "dg", q.dma_start(out=g.gbrow_bf[:, :], in_=g.l0f_gb), 16)
                inc(g, "dg", q.dma_start(out=g.l0fb_bf[:, :], in_=g.l0f_b), 16)
                if phases >= 4:
                    inc(g, "dg", q.dma_start(out=g.l1gw_sb[:, :], in_=g.l1f_gw), 16)
            b.gpsimd(f_init_gdma)

            ds_init = g.cnt["ds"]

            def f_init_v(v):
                v.memset(g.ones_bf[:, :], 1.0)
                v.memset(g.ones1_f32[:, :], 1.0)
                v.memset(g.zero64[:, :], 0.0)
                v.memset(g.halfpi[:, :], float(np.pi / 2))
                v.wait_ge(g.sem["ds"], ds_init)
                inc(g, "vd", v.tensor_scalar(out=g.m1col[:, :], in0=g.mcol[:, :],
                                             scalar1=-1.0, scalar2=1.0,
                                             op0=ALU.mult, op1=ALU.add))
            b.vector(f_init_v)

            if phases >= 5:
                vd_i = g.cnt["vd"]

                def f_sp(s):
                    s.wait_ge(g.sem["vd"], vd_i)
                    inc(g, "sd", s.activation(out=g.m8sp[:, :], in_=g.apc[:, :],
                                              func=AF.Exp))
                b.scalar(f_sp)
                sd_i = g.cnt["sd"]

                def f_sp2(v):
                    v.wait_ge(g.sem["sd"], sd_i)
                    inc(g, "vd", v.tensor_scalar(out=g.m8sp[:, :],
                                                 in0=g.m8sp[:, :], scalar1=1.0,
                                                 scalar2=None, op0=ALU.add))
                b.vector(f_sp2)
                vd_i2 = g.cnt["vd"]

                def f_sp3(s):
                    s.wait_ge(g.sem["vd"], vd_i2)
                    inc(g, "sd", s.activation(out=g.m8sp[:, :], in_=g.m8sp[:, :],
                                              func=AF.Ln))
                b.scalar(f_sp3)
                sd_i2 = g.cnt["sd"]

                def f_sp4(v):
                    v.wait_ge(g.sem["sd"], sd_i2)
                    inc(g, "vd", v.tensor_scalar(out=g.m8sp[:, :],
                                                 in0=g.m8sp[:, :], scalar1=-8.0,
                                                 scalar2=None, op0=ALU.mult))
                b.vector(f_sp4)

            join(g)

            seq = []
            if phases >= 1:
                seq.append(("P1_l0ffn", emit_l0ffn))
            if phases >= 2:
                seq.append(("P2_l0attn", emit_l0attn))
            if phases >= 3:
                seq.append(("P3_l0ffn", emit_l0ffn))
            if phases >= 4:
                seq.append(("P4_l1ffn", emit_l1ffn))
            if phases >= 5:
                seq.append(("P5_l1attn", emit_l1attn))
            if phases >= 6:
                seq.append(("P6_l1ffn", emit_l1ffn))
            for nm, fn in seq:
                with nc.named_scope(nm):
                    fn(g)

            def f_store(q):
                for pt in range(PT):
                    inc(g, "ds", q.dma_start(out=g.x_out[pt * 128:(pt + 1) * 128, :],
                                             in_=g.xs[pt][:, :]), 16)
                q.wait_ge(g.sem["ds"], g.cnt["ds"])
            b.sync(f_store)
    return g


def _colmajor(v):
    v = np.ascontiguousarray(np.asarray(v, dtype=np.float32))
    n = v.shape[0] // 128
    return np.ascontiguousarray(v.reshape(n, 128).T)


def _prep_shared(inputs, phases):
    W = {}
    W["eye"] = np.eye(128, dtype=np.float32)
    oh8 = np.zeros((8, 1024), np.float32)
    for e in range(8):
        oh8[e, e * 128:(e + 1) * 128] = 1.0
    W["oh8"] = oh8
    ew = np.asarray(inputs["L0_ffn_expert_w"], np.float32)
    W["l0f_w"] = np.ascontiguousarray(
        ew.reshape(8, 8, 128, 1024).transpose(0, 2, 1, 3).reshape(8 * 128, 8192))
    W["l0f_b"] = np.ascontiguousarray(np.asarray(inputs["L0_ffn_expert_b"], np.float32))
    gw = np.asarray(inputs["L0_ffn_gate_w"], np.float32)
    W["l0f_gw"] = np.ascontiguousarray(gw.reshape(8, 128, 8).transpose(1, 0, 2).reshape(128, 64))
    W["l0f_gb"] = np.ascontiguousarray(np.asarray(inputs["L0_ffn_gate_b"], np.float32).reshape(1, 8))
    if phases >= 2:
        w1T = np.asarray(inputs["L0_attn_w1"], np.float32).T
        w1p = np.zeros((128, 2048), np.float32)
        for mt in range(32):
            w1p[(mt % 2) * 64:(mt % 2) * 64 + 16,
                (mt // 2) * 128:(mt // 2) * 128 + 128] = w1T[:, mt * 128:(mt + 1) * 128]
        W["l0a_w1t"] = w1p
        w2t = np.asarray(inputs["L0_attn_w2"], np.float32).T
        W["l0a_w2t"] = np.ascontiguousarray(
            w2t.reshape(32, 128, 1024).transpose(1, 0, 2).reshape(128, 32768))
        W["l0a_b1"] = _colmajor(inputs["L0_attn_b1"])
        W["l0a_b2"] = _colmajor(inputs["L0_attn_b2"])
        W["l0a_g1"] = _colmajor(inputs["L0_attn_gamma1"])
        W["l0a_bt1"] = _colmajor(inputs["L0_attn_beta1"])
        W["l0a_g2"] = _colmajor(inputs["L0_attn_gamma2"])
        W["l0a_bt2"] = _colmajor(inputs["L0_attn_beta2"])
    def _slab(a, nk, m):
        a = np.asarray(a, np.float32).reshape(8, nk, 128, m)
        return np.ascontiguousarray(a.transpose(0, 2, 1, 3).reshape(8 * 128, nk * m))
    if phases >= 4:
        gw1 = np.asarray(inputs["L1_ffn_w_gate"], np.float32)
        W["l1f_gw"] = np.ascontiguousarray(gw1.reshape(8, 128, 8).transpose(1, 0, 2).reshape(128, 64))
        W["l1f_wg"] = _slab(inputs["L1_ffn_w_g"], 8, 1408)
        W["l1f_wu"] = _slab(inputs["L1_ffn_w_u"], 8, 1408)
        W["l1f_wd"] = _slab(inputs["L1_ffn_w_d"], 11, 1024)
    if phases >= 5:
        W["l1a_wx"] = np.ascontiguousarray(np.asarray(inputs["L1_attn_w_x"], np.float32).reshape(8, 128, 1024).transpose(1, 0, 2).reshape(128, 8192))
        W["l1a_wy"] = np.ascontiguousarray(np.asarray(inputs["L1_attn_w_y"], np.float32).reshape(8, 128, 1024).transpose(1, 0, 2).reshape(128, 8192))
        W["l1a_gi"] = np.ascontiguousarray(np.asarray(inputs["L1_attn_gi_w"], np.float32).reshape(8, 128, 1024).transpose(1, 0, 2).reshape(128, 8192))
        W["l1a_ga"] = np.ascontiguousarray(np.asarray(inputs["L1_attn_ga_w"], np.float32).reshape(8, 128, 1024).transpose(1, 0, 2).reshape(128, 8192))
        W["l1a_bx"] = _colmajor(inputs["L1_attn_b_x"])
        W["l1a_by"] = _colmajor(inputs["L1_attn_b_y"])
        W["l1a_gib"] = _colmajor(inputs["L1_attn_gi_b"])
        W["l1a_gab"] = _colmajor(inputs["L1_attn_ga_b"])
        W["l1a_cb"] = _colmajor(inputs["L1_attn_conv_b"])
        cw = np.asarray(inputs["L1_attn_conv_w"], np.float32)
        W["l1a_cw"] = np.ascontiguousarray(cw.reshape(4, 8, 128).transpose(2, 0, 1).reshape(128, 32))
        W["l1a_ap"] = _colmajor(inputs["L1_attn_a_param"])
    return W


def kernel(**inputs):
    phases = PHASES
    assert np.asarray(inputs["L0_attn_attn_circuit_params"]).shape == (16, 64)
    assert np.asarray(inputs["L0_attn_ffn_circuit_params"]).shape == (16,)
    x = np.asarray(inputs["L0_attn_x"], np.float32)
    xt = np.ascontiguousarray(x[0].T)
    W = _prep_shared(inputs, phases)
    in_maps = []
    for k in range(NCORES):
        d = dict(W)
        d["x_in"] = np.ascontiguousarray(xt[:, k * T:(k + 1) * T])
        d["mcol"] = np.full((128, 1), 0.0 if k == 0 else 1.0, np.float32)
        oh = np.zeros((128, 8), np.float32)
        if k > 0:
            oh[:, k - 1] = 1.0
        d["onehot"] = oh
        d["prev_idx"] = np.array([[(k - 1) % NCORES]], np.uint32)
        in_maps.append(d)
    g = build_program(phases)
    res = run_bass_kernel_spmd(g.nc, in_maps, core_ids=list(range(NCORES)))
    out = np.empty((1, 16384, 1024), np.float32)
    for k in range(NCORES):
        out[0, k * T:(k + 1) * T, :] = res.results[k]["x_out"].T
    return out


# version 17 (final)
# speedup vs baseline: 1.1783x; 1.0391x over previous
import numpy as np
from contextlib import ExitStack

import concourse.bass as bass
import concourse.mybir as mybir
from concourse.bass_utils import run_bass_kernel_spmd

F32 = mybir.dt.float32
BF16 = mybir.dt.bfloat16
U32 = mybir.dt.uint32
AF = mybir.ActivationFunctionType
ALU = mybir.AluOpType
AX = mybir.AxisListType.X

NCORES = 8
T = 2048
D = 1024
PT = 8
EPS = 1e-5
MAGIC = 12582912.0
TWO_PI = float(2 * np.pi)
C1 = float(np.float32(TWO_PI))
C2 = float(TWO_PI - C1)
INV2PI = float(1.0 / TWO_PI)

PHASES = 6
DEBUG_TAPS = False
DEBUG_TAP_TAG = 1


class G:
    pass


def inc(g, name, ins, by=1):
    ins.then_inc(g.sem[name], by)
    g.cnt[name] += by
    return g.cnt[name]


MAX_TBL_RUNS = 1


def sdefer(g, n, build, tbl='c'):
    if g.sc_pend and g.sc_last != tbl and g.sc_runs >= MAX_TBL_RUNS:
        sflush(g)
    if not g.sc_pend:
        g.sc_runs = 1
    elif g.sc_last != tbl:
        g.sc_runs += 1
    g.sc_last = tbl
    g.cnt["sd"] += n
    g.sc_pend.append(build)
    return g.cnt["sd"]


def sflush(g):
    if not g.sc_pend:
        return
    items = g.sc_pend
    g.sc_pend = []
    g.sc_runs = 0
    g.sc_last = None

    def f(s):
        for it in items:
            it(s)
    g.block.scalar(f)


def join(g):
    sflush(g)
    snap = {k: v for k, v in g.cnt.items() if v > 0}

    def mk(eng):
        for s, v in snap.items():
            eng.wait_ge(g.sem[s], v)

    b = g.block
    b.sync(mk); b.vector(mk); b.scalar(mk); b.tensor(mk); b.gpsimd(mk)


def emit_ln(g, lns, srcs, outs, W, gamma=None, beta=None, pso=0, entry_vd=None):
    b, sem, cnt = g.block, g.sem, g.cnt
    full = W > 512
    if full:
        reg_s1 = lambda lo, w: g.psA[:, lo:lo + w]
        reg_s2 = lambda lo, w: g.psB[:, lo:lo + w]
        reg_bi, reg_bm = reg_s1, reg_s2
        row_s1 = g.psA[0:1, 0:W]
        row_s2 = g.psB[0:1, 0:W]
    else:
        reg_s1 = lambda lo, w: g.psA[:, pso + lo:pso + lo + w]
        reg_s2 = lambda lo, w: g.psA[:, pso + 256 + lo:pso + 256 + lo + w]
        reg_bi = reg_s1
        reg_bm = reg_s2
        row_s1 = g.psA[0:1, pso:pso + W]
        row_s2 = g.psA[0:1, pso + 256:pso + 256 + W]
    nchk = (W + 511) // 512
    cw = [min(512, W - i * 512) for i in range(nchk)]

    sd0 = cnt["sd"]
    td0 = cnt["td"]
    vd_entry = cnt["vd"] if entry_vd is None else entry_vd

    def b_copy(s):
        s.wait_ge(sem["vd"], vd_entry)
        for pt in range(PT):
            if pt >= 2:
                s.wait_ge(sem["td"], td0 + pt - 1)
            s.copy(out=lns['lnA'][pt % 2][:, 0:W],
                   in_=srcs[pt]).then_inc(sem["sd"], 1)
    sdefer(g, PT, b_copy, 'c')

    vsq0 = cnt["vd"]

    def f_vsq(v):
        v.wait_ge(sem["vd"], vd_entry)
        for pt in range(PT):
            if pt >= 2:
                v.wait_ge(sem["td"], td0 + pt - 1)
            inc(g, "vd", v.tensor_tensor(out=lns['lnB'][pt % 2][:, 0:W],
                                         in0=srcs[pt], in1=srcs[pt],
                                         op=ALU.mult))
    b.vector(f_vsq)

    def f_pe(t):
        t.wait_ge(sem["vd"], vd_entry)
        for pt in range(PT):
            t.wait_ge(sem["sd"], sd0 + pt + 1)
            t.wait_ge(sem["vd"], vsq0 + pt + 1)
            last = None
            for i in range(nchk):
                lo = i * 512
                t.matmul(reg_s1(lo, cw[i]), g.ones_bf[:, :],
                         lns['lnA'][pt % 2][:, lo:lo + cw[i]],
                         start=(pt == 0), stop=(pt == PT - 1))
                last = t.matmul(reg_s2(lo, cw[i]), g.ones_bf[:, :],
                                lns['lnB'][pt % 2][:, lo:lo + cw[i]],
                                start=(pt == 0), stop=(pt == PT - 1))
            inc(g, "td", last)
    b.tensor(f_pe)
    td_stats = cnt["td"]

    def f_v1(v):
        def settle(ins):
            v.wait_ge(sem["vd"], inc(g, "vd", ins))
        v.wait_ge(sem["td"], td_stats)
        settle(v.tensor_scalar(out=lns['row_m'][:, 0:W], in0=row_s1,
                               scalar1=1.0 / D, scalar2=None, op0=ALU.mult))
        settle(v.tensor_scalar(out=lns['row_v'][:, 0:W], in0=row_s2,
                               scalar1=1.0 / D, scalar2=EPS,
                               op0=ALU.mult, op1=ALU.add))
        settle(v.tensor_tensor(out=lns['lnT'][0:1, 0:W], in0=lns['row_m'][:, 0:W],
                               in1=lns['row_m'][:, 0:W], op=ALU.mult))
        settle(v.tensor_tensor(out=lns['row_v'][:, 0:W], in0=lns['row_v'][:, 0:W],
                               in1=lns['lnT'][0:1, 0:W], op=ALU.subtract))
        inc(g, "vd", v.reciprocal(out=lns['row_v'][:, 0:W],
                                  in_=lns['row_v'][:, 0:W]))
    b.vector(f_v1)
    vd2 = cnt["vd"]

    def b_s3(s):
        s.wait_ge(sem["vd"], vd2)
        s.activation(out=lns['row_v'][:, 0:W], in_=lns['row_v'][:, 0:W],
                     func=AF.Sqrt).then_inc(sem["sd"], 1)
    sd3 = sdefer(g, 1, b_s3, 'sqrt')

    def f_v3(v):
        v.wait_ge(sem["sd"], sd3)
        inc(g, "vd", v.tensor_tensor(out=lns['row_m'][:, 0:W], in0=lns['row_m'][:, 0:W],
                                     in1=lns['row_v'][:, 0:W], op=ALU.mult))
    b.vector(f_v3)
    vd3 = cnt["vd"]

    def f_pe2(t):
        t.wait_ge(sem["vd"], vd3)
        last = None
        for i in range(nchk):
            lo = i * 512
            t.matmul(reg_bi(lo, cw[i]), g.ones1_f32[:, :],
                     lns['row_v'][:, lo:lo + cw[i]], start=True, stop=True)
            last = t.matmul(reg_bm(lo, cw[i]), g.ones1_f32[:, :],
                            lns['row_m'][:, lo:lo + cw[i]], start=True, stop=True)
        inc(g, "td", last)
    b.tensor(f_pe2)
    td_bc = cnt["td"]

    def f_v4(v):
        v.wait_ge(sem["td"], td_bc)
        last = None
        for pt in range(PT):
            v.tensor_tensor(out=lns['lnT'][:, 0:W], in0=srcs[pt], in1=reg_bi(0, W),
                            op=ALU.mult)
            last = v.tensor_tensor(out=outs[pt], in0=lns['lnT'][:, 0:W],
                                   in1=reg_bm(0, W), op=ALU.subtract)
            if gamma is not None:
                last = v.tensor_scalar(out=outs[pt], in0=outs[pt],
                                       scalar1=gamma[pt], scalar2=beta[pt],
                                       op0=ALU.mult, op1=ALU.add)
        inc(g, "vd", last)
    b.vector(f_v4)


def emit_gating(g, t0, gw_sb, mode):
    b, sem, cnt = g.block, g.sem, g.cnt
    vd_entry = cnt["vd"]
    dg_entry = cnt["dg"]

    def f_pe(t):
        t.wait_ge(sem["vd"], vd_entry)
        t.wait_ge(sem["dg"], dg_entry)
        last = None
        for ch in range(16):
            reg = g.psA[:, ch * 8:ch * 8 + 8]
            for kt in range(PT):
                last = t.matmul(reg, t0[kt][:, ch * 128:(ch + 1) * 128],
                                gw_sb[:, kt * 8:(kt + 1) * 8],
                                start=(kt == 0),
                                stop=(kt == 7 and mode == "l1"))
            if mode == "l0":
                last = t.matmul(reg, g.ones_bf[0:1, :], g.gbrow_bf[:, :],
                                start=False, stop=True)
        inc(g, "td", last)
    b.tensor(f_pe)
    td_log = cnt["td"]

    if mode == "l0" and DEBUG_TAPS and g.tag == DEBUG_TAP_TAG:
        def f_tapv(v):
            v.wait_ge(sem["td"], td_log)
            inc(g, "vd", v.tensor_copy(out=g.gis1[:, :], in_=g.psA[:, 0:128]))
        b.vector(f_tapv)
        vd_tap = cnt["vd"]
        def f_tapd(q):
            q.wait_ge(sem["vd"], vd_tap)
            inc(g, "ds", q.dma_start(out=g.tap_lg, in_=g.gis1[:, :]), 16)
        b.sync(f_tapd)

    if mode == "l0":
        def f_v0(v):
            v.wait_ge(sem["td"], td_log)
            last = None
            for ch in range(16):
                last = v.tensor_reduce(out=g.gnm[:, ch:ch + 1],
                                       in_=g.psA[:, ch * 8:ch * 8 + 8],
                                       axis=AX, op=ALU.max, negate=True)
            inc(g, "vd", last)
        b.vector(f_v0)
        vd_nm = cnt["vd"]

        def b_s(s):
            s.wait_ge(sem["vd"], vd_nm)
            for ch in range(16):
                last = s.activation(out=g.gsc[:, ch * 8:ch * 8 + 8],
                                    in_=g.psA[:, ch * 8:ch * 8 + 8],
                                    func=AF.Exp, bias=g.gnm[:, ch:ch + 1])
            last.then_inc(sem["sd"], 1)
        sd_exp = sdefer(g, 1, b_s, 'exp')

        if DEBUG_TAPS and g.tag == DEBUG_TAP_TAG:
            def f_tapd2(q):
                q.wait_ge(sem["sd"], sd_exp)
                inc(g, "ds", q.dma_start(out=g.tap_nm, in_=g.gnm[:, :]), 16)
                inc(g, "ds", q.dma_start(out=g.tap_ex, in_=g.gsc[:, :]), 16)
            b.sync(f_tapd2)

        def f_v2(v):
            def settle(ins):
                v.wait_ge(sem["vd"], inc(g, "vd", ins))
            v.wait_ge(sem["sd"], sd_exp)
            last = None
            for ch in range(16):
                e = g.gsc[:, ch * 8:ch * 8 + 8]
                settle(v.tensor_reduce(out=g.colA[:, :], in_=e, axis=AX,
                                       op=ALU.add))
                settle(v.tensor_scalar(out=g.colB[:, :], in0=g.colA[:, :],
                                       scalar1=0.1, scalar2=None, op0=ALU.mult))
                settle(v.tensor_scalar(out=g.gs2[:, ch * 8:ch * 8 + 8], in0=e,
                                       scalar1=g.colB[:, :], op0=ALU.is_ge,
                                       scalar2=None))
                settle(v.tensor_tensor(out=g.gs2[:, ch * 8:ch * 8 + 8], in0=e,
                                       in1=g.gs2[:, ch * 8:ch * 8 + 8],
                                       op=ALU.mult))
                settle(v.tensor_reduce(out=g.colA[:, :],
                                       in_=g.gs2[:, ch * 8:ch * 8 + 8], axis=AX,
                                       op=ALU.add))
                settle(v.reciprocal(out=g.colA[:, :], in_=g.colA[:, :]))
                last = v.tensor_scalar(out=g.gatw[:, ch * 8:ch * 8 + 8],
                                       in0=g.gs2[:, ch * 8:ch * 8 + 8],
                                       scalar1=g.colA[:, :], scalar2=None,
                                       op0=ALU.mult)
            inc(g, "vd", last)
        b.vector(f_v2)
    else:
        def f_v1(v):
            v.wait_ge(sem["td"], td_log)
            last = None
            def settle(ins):
                v.wait_ge(sem["vd"], inc(g, "vd", ins))
            for ch in range(16):
                lg = g.psA[:, ch * 8:ch * 8 + 8]
                settle(v.tensor_reduce(out=g.colA[:, :], in_=lg, axis=AX,
                                       op=ALU.max))
                settle(v.tensor_scalar(out=g.gis1[:, ch * 8:ch * 8 + 8], in0=lg,
                                       scalar1=g.colA[:, :], op0=ALU.is_ge,
                                       scalar2=None))
                settle(v.scalar_tensor_tensor(out=g.gs2[:, ch * 8:ch * 8 + 8],
                                              in0=g.gis1[:, ch * 8:ch * 8 + 8],
                                              scalar=-1.0e9, in1=lg,
                                              op0=ALU.mult, op1=ALU.add))
                settle(v.tensor_reduce(out=g.colB[:, :],
                                       in_=g.gs2[:, ch * 8:ch * 8 + 8], axis=AX,
                                       op=ALU.max))
                settle(v.tensor_scalar(out=g.gis2[:, ch * 8:ch * 8 + 8],
                                       in0=g.gs2[:, ch * 8:ch * 8 + 8],
                                       scalar1=g.colB[:, :], op0=ALU.is_ge,
                                       scalar2=None))
                last = v.tensor_tensor(out=g.colC[:, ch:ch + 1],
                                       in0=g.colA[:, :], in1=g.colB[:, :],
                                       op=ALU.subtract)
            inc(g, "vd", last)
        b.vector(f_v1)
        vd_pre = cnt["vd"]

        def b_sg(s):
            s.wait_ge(sem["vd"], vd_pre)
            s.activation(out=g.colP[:, 0:16], in_=g.colC[:, 0:16],
                         func=AF.Sigmoid).then_inc(sem["sd"], 1)
        sd_sig = sdefer(g, 1, b_sg, 'sig')

        def f_v2(v):
            v.wait_ge(sem["sd"], sd_sig)
            last = None
            def settle(ins):
                v.wait_ge(sem["vd"], inc(g, "vd", ins))
            for ch in range(16):
                p1 = g.colP[:, ch:ch + 1]
                settle(v.tensor_scalar(out=g.colB[:, :], in0=p1, scalar1=-1.0,
                                       scalar2=1.0, op0=ALU.mult, op1=ALU.add))
                settle(v.tensor_scalar(out=g.gs2[:, ch * 8:ch * 8 + 8],
                                       in0=g.gis2[:, ch * 8:ch * 8 + 8],
                                       scalar1=g.colB[:, :], scalar2=None,
                                       op0=ALU.mult))
                last = v.scalar_tensor_tensor(out=g.gatw[:, ch * 8:ch * 8 + 8],
                                              in0=g.gis1[:, ch * 8:ch * 8 + 8],
                                              scalar=p1,
                                              in1=g.gs2[:, ch * 8:ch * 8 + 8],
                                              op0=ALU.mult, op1=ALU.add)
            inc(g, "vd", last)
        b.vector(f_v2)

    vd_w = cnt["vd"]

    def f_pe2(t):
        t.wait_ge(sem["vd"], vd_w)
        last = None
        for ch in range(16):
            last = t.transpose(g.psB[0:8, ch * 128:(ch + 1) * 128],
                               g.gatw[:, ch * 8:ch * 8 + 8], g.eye32[:, :])
        inc(g, "td", last)
    b.tensor(f_pe2)
    td_tr = cnt["td"]

    def b_s2(s):
        s.wait_ge(sem["td"], td_tr)
        s.copy(out=g.wT8[:, :], in_=g.psB[0:8, 0:2048]).then_inc(sem["sd"], 1)
    sd_w = sdefer(g, 1, b_s2, 'c')

    g.sd_wT8 = cnt["sd"]

    if DEBUG_TAPS and g.tag == DEBUG_TAP_TAG:
        def f_tapg(q):
            q.wait_ge(sem["sd"], g.sd_wT8)
            q.wait_ge(sem["vd"], cnt["vd"])
            inc(g, "ds", q.dma_start(out=g.tap_gw, in_=g.gatw[:, :]), 16)
            inc(g, "ds", q.dma_start(out=g.tap_wT8, in_=g.wT8[:, :]), 16)
        b.sync(f_tapg)


def alloc_ln(g, ph, W, tag, shared_lnT=None):
    nc = g.nc
    sbc = lambda nm, sh, dt=F32: ph.enter_context(nc.sbuf_tensor(nm, sh, dt))
    return {
        "lnA": [sbc(f"{tag}_lnA{i}", [128, W], BF16) for i in range(2)],
        "lnB": [sbc(f"{tag}_lnB{i}", [128, W], BF16) for i in range(2)],
        "lnT": shared_lnT if shared_lnT is not None else sbc(f"{tag}_lnT", [128, W]),
        "row_m": sbc(f"{tag}_rm", [1, W]),
        "row_v": sbc(f"{tag}_rv", [1, W]),
    }


def emit_l0ffn(g):
    b, sem, cnt, nc = g.block, g.sem, g.cnt, g.nc
    g.tag += 1
    tg = g.tag
    with ExitStack() as ph:
        g.wT8 = ph.enter_context(nc.sbuf_tensor(f"ph{tg}_wT8", [8, 2048], BF16))
        t0 = [ph.enter_context(nc.sbuf_tensor(f"ph{tg}_t0_{i}", [128, T], BF16))
              for i in range(PT)]
        t0e = [ph.enter_context(nc.sbuf_tensor(f"ph{tg}_te_{i}", [128, 512], BF16))
               for i in range(PT)]
        wsl = [ph.enter_context(nc.sbuf_tensor(f"ph{tg}_w_{i}", [128, 8192], BF16))
               for i in range(2)]

        with ExitStack() as lnph:
            lns = alloc_ln(g, lnph, T, f"ph{tg}ln")
            emit_ln(g, lns, [g.xs[pt][:, :] for pt in range(PT)],
                    [t0[pt][:, :] for pt in range(PT)], T)
            emit_gating(g, t0, g.l0gw_sb, "l0")

            if DEBUG_TAPS and tg == DEBUG_TAP_TAG:
                vd_now = cnt["vd"]; sd_now = cnt["sd"]
                def f_tap(q):
                    q.wait_ge(sem["vd"], vd_now)
                    q.wait_ge(sem["sd"], sd_now)
                    for pt in range(PT):
                        inc(g, "ds", q.dma_start(out=g.tap_t0[pt * 128:(pt + 1) * 128, :],
                                                 in_=t0[pt][:, :]), 16)
                    inc(g, "ds", q.dma_start(out=g.tap_gw, in_=g.gatw[:, :]), 16)
                    inc(g, "ds", q.dma_start(out=g.tap_wT8, in_=g.wT8[:, :]), 16)
                b.sync(f_tap)

            mm_td = {}
            ev_vd = {}
            mm_done = {}
            scale_vd = {}
            scale0 = {}
            bc_td = {}
            row_dg = {}

            gidx_of = lambda e, n, mt: (e * 4 + n) * 8 + mt

            for e in range(8):
                def f_dma(q, e=e):
                    if e >= 2:
                        q.wait_ge(sem["td"], mm_done[e - 2])
                    row_dg[e] = inc(
                        g, "dg", q.dma_start(out=wsl[e % 2][:, :],
                                             in_=g.l0f_w[e * 128:(e + 1) * 128, :]), 16)
                b.gpsimd(f_dma)

                def f_bc(t, e=e):
                    if e == 0:
                        t.wait_ge(sem["vd"], cnt["vd"])
                        t.wait_ge(sem["sd"], g.sd_wT8)
                    else:
                        t.wait_ge(sem["vd"], scale_vd[(e - 1, 3)])
                    last = None
                    for i in range(4):
                        last = t.matmul(
                            g.psA[:, i * 512:(i + 1) * 512],
                            g.oh8_bf[:, e * 128:(e + 1) * 128],
                            g.wT8[:, i * 512:(i + 1) * 512],
                            start=True, stop=True)
                    bc_td[e] = inc(g, "td", last)
                b.tensor(f_bc)

                for n in range(4):
                    scale0[(e, n)] = cnt["vd"]

                    def f_sc(v, e=e, n=n):
                        v.wait_ge(sem["td"], bc_td[e])
                        if e > 0 or n > 0:
                            pe, pn = (e, n - 1) if n > 0 else (e - 1, 3)
                            v.wait_ge(sem["td"], mm_td[gidx_of(pe, pn, 7)])
                        for kt in range(PT):
                            inc(g, "vd", v.tensor_tensor(
                                out=t0e[kt][:, :],
                                in0=t0[kt][:, n * 512:(n + 1) * 512],
                                in1=g.psA[:, n * 512:(n + 1) * 512], op=ALU.mult))
                        scale_vd[(e, n)] = cnt["vd"]
                    b.vector(f_sc)

                    for mt in range(PT):
                        def f_mm(t, e=e, n=n, mt=mt):
                            gidx = gidx_of(e, n, mt)
                            slot = gidx % 4
                            reg = g.psB[:, slot * 512:(slot + 1) * 512]
                            if mt == 0:
                                t.wait_ge(sem["dg"], row_dg[e])
                            if gidx >= 4:
                                t.wait_ge(sem["vd"], ev_vd[gidx - 4])
                            nb = (e == 0)
                            last = None
                            for kt in range(PT):
                                if mt == 0:
                                    t.wait_ge(sem["vd"], scale0[(e, n)] + kt + 1)
                                last = t.matmul(
                                    reg,
                                    wsl[e % 2][:, (kt * 8 + mt) * 128:(kt * 8 + mt + 1) * 128],
                                    t0e[kt][:, :],
                                    start=(kt == 0), stop=(kt == 7 and not nb))
                            if nb:
                                last = t.matmul(reg,
                                                g.l0fb_bf[:, mt * 128:(mt + 1) * 128],
                                                g.wT8[:, n * 512:(n + 1) * 512],
                                                start=False, stop=True)
                            mm_td[gidx_of(e, n, mt)] = inc(g, "td", last)
                        b.tensor(f_mm)

                        def f_ev(v, e=e, n=n, mt=mt):
                            gidx = gidx_of(e, n, mt)
                            slot = gidx % 4
                            v.wait_ge(sem["td"], mm_td[gidx])
                            last = v.scalar_tensor_tensor(
                                out=g.xs[mt][:, n * 512:(n + 1) * 512],
                                in0=g.psB[:, slot * 512:(slot + 1) * 512],
                                scalar=0.5,
                                in1=g.xs[mt][:, n * 512:(n + 1) * 512],
                                op0=ALU.mult, op1=ALU.add)
                            ev_vd[gidx] = inc(g, "vd", last)
                        b.vector(f_ev)
                mm_done[e] = cnt["td"]
        join(g)



def emit_l0attn(g):
    b, sem, cnt, nc = g.block, g.sem, g.cnt, g.nc
    g.tag += 1
    tg = g.tag
    W = 256
    with ExitStack() as ph:
        sbc = lambda nm, sh, dt=F32: ph.enter_context(nc.sbuf_tensor(f"ph{tg}_{nm}", sh, dt))
        w1t = sbc("w1t", [128, 2048], BF16)
        w2t = sbc("w2t", [128, 32 * 1024], BF16)
        tc2 = [[sbc(f"tc{p}_{i}", [128, W]) for i in range(PT)] for p in range(2)]
        x1c2 = [[sbc(f"x1c{p}_{i}", [128, W]) for i in range(PT)] for p in range(2)]
        hc = [sbc(f"hc{i}", [128, 1024], BF16) for i in range(PT)]
        qa = sbc("qa", [128, 1024]); qb = sbc("qb", [128, 1024])
        qc = qb
        qd = qa
        qf2 = [sbc(f"qf{p}", [128, W], BF16) for p in range(2)]
        lns2 = [alloc_ln(g, ph, W, f"ph{tg}ln0")]
        lns2.append(alloc_ln(g, ph, W, f"ph{tg}ln1", shared_lnT=lns2[0]["lnT"]))

        def f_wdma(q):
            inc(g, "dg", q.dma_start(out=w1t[:, :], in_=g.l0a_w1t), 16)
            inc(g, "dg", q.dma_start(out=w2t[:, :], in_=g.l0a_w2t), 16)
        b.gpsimd(f_wdma)
        dg_w = cnt["dg"]

        hmm_td = {}
        hev_sd = {}
        fmm_td = {}
        fev_vd = {}
        vd_qf = {}

        def emit_front(ci):
            par = ci % 2
            pso = par * 512
            c0 = ci * W
            tc = tc2[par]
            x1c = x1c2[par]
            qf = qf2[par]
            lns = lns2[par]
            xsrc = [g.xs[pt][:, c0:c0 + W] for pt in range(PT)]
            emit_ln(g, lns, xsrc, [tc[pt][:, :] for pt in range(PT)], W, pso=pso)

            for tt in range(2):
                lo = tt * 128
                def f_tr(t, tt=tt, lo=lo):
                    t.wait_ge(sem["vd"], cnt["vd"])
                    last = None
                    for pt in range(PT):
                        last = t.transpose(g.psB[:, 1024 + pt * 128:1024 + (pt + 1) * 128],
                                           tc[pt][:, lo:lo + 128], g.eye32[:, :])
                    inc(g, "td", last)
                b.tensor(f_tr)
                td_tr = cnt["td"]

                def f_rr(v, td_tr=td_tr):
                    def settle(ins):
                        v.wait_ge(sem["vd"], inc(g, "vd", ins))
                    v.wait_ge(sem["td"], td_tr)
                    settle(v.tensor_scalar(out=qa[:, :], in0=g.psB[:, 1024:2048],
                                           scalar1=INV2PI, scalar2=0.25,
                                           op0=ALU.mult, op1=ALU.add))
                    settle(v.tensor_scalar(out=qa[:, :], in0=qa[:, :],
                                           scalar1=MAGIC, scalar2=MAGIC,
                                           op0=ALU.add, op1=ALU.subtract))
                    settle(v.scalar_tensor_tensor(out=qb[:, :], in0=qa[:, :],
                                                  scalar=-C1, in1=g.psB[:, 1024:2048],
                                                  op0=ALU.mult, op1=ALU.add))
                    settle(v.scalar_tensor_tensor(out=qb[:, :], in0=qa[:, :],
                                                  scalar=-C2, in1=qb[:, :],
                                                  op0=ALU.mult, op1=ALU.add))
                b.vector(f_rr)
                vd_rr = cnt["vd"]

                def b_cos(s, vd_rr=vd_rr):
                    s.wait_ge(sem["vd"], vd_rr)
                    s.activation(out=qc[:, :], in_=qb[:, :],
                                 func=AF.Sin, bias=g.halfpi[:, :]).then_inc(sem["sd"], 1)
                sd_cos = sdefer(g, 1, b_cos, 'sin')

                def f_scan(v, sd_cos=sd_cos):
                    v.wait_ge(sem["sd"], sd_cos)
                    last = None
                    for h in range(16):
                        last = v.tensor_tensor_scan(
                            out=qd[:, h * 64:(h + 1) * 64],
                            data0=qc[:, h * 64:(h + 1) * 64],
                            data1=g.zero64[:, 0:64], initial=1.0,
                            op0=ALU.mult, op1=ALU.add)
                    inc(g, "vd", last)
                b.vector(f_scan)
                vd_scan = cnt["vd"]

                def f_tr2(t, vd_scan=vd_scan):
                    t.wait_ge(sem["vd"], vd_scan)
                    last = None
                    for pt in range(PT):
                        last = t.transpose(g.psB[:, 1024 + pt * 128:1024 + (pt + 1) * 128],
                                           qd[:, pt * 128:(pt + 1) * 128],
                                           g.eye32[:, :])
                    inc(g, "td", last)
                b.tensor(f_tr2)
                td_tr2 = cnt["td"]

                def f_add(v, tt=tt, lo=lo, td_tr2=td_tr2):
                    v.wait_ge(sem["td"], td_tr2)
                    last = None
                    for pt in range(PT):
                        last = v.tensor_tensor(
                            out=tc[pt][:, lo:lo + 128], in0=tc[pt][:, lo:lo + 128],
                            in1=g.psB[:, 1024 + pt * 128:1024 + (pt + 1) * 128],
                            op=ALU.add)
                    inc(g, "vd", last)
                b.vector(f_add)

            emit_ln(g, lns, [tc[pt][:, :] for pt in range(PT)],
                    [x1c[pt][:, :] for pt in range(PT)], W, pso=pso,
                    gamma=[g.g1c[:, pt:pt + 1] for pt in range(PT)],
                    beta=[g.bt1c[:, pt:pt + 1] for pt in range(PT)])

            for tt in range(2):
                lo = tt * 128
                def f_q2tr(t, lo=lo):
                    t.wait_ge(sem["vd"], cnt["vd"])
                    inc(g, "td", t.transpose(g.psB[:, 1024:1040],
                                             x1c[0][0:16, lo:lo + 128],
                                             g.eye32[0:16, 0:16]))
                b.tensor(f_q2tr)
                td_q = cnt["td"]

                def f_q2rr(v, td_q=td_q):
                    def settle(ins):
                        v.wait_ge(sem["vd"], inc(g, "vd", ins))
                    v.wait_ge(sem["td"], td_q)
                    settle(v.tensor_scalar(out=qa[:, 0:16], in0=g.psB[:, 1024:1040],
                                           scalar1=INV2PI, scalar2=0.25,
                                           op0=ALU.mult, op1=ALU.add))
                    settle(v.tensor_scalar(out=qa[:, 0:16], in0=qa[:, 0:16],
                                           scalar1=MAGIC, scalar2=MAGIC,
                                           op0=ALU.add, op1=ALU.subtract))
                    settle(v.scalar_tensor_tensor(out=qb[:, 0:16], in0=qa[:, 0:16],
                                                  scalar=-C1, in1=g.psB[:, 1024:1040],
                                                  op0=ALU.mult, op1=ALU.add))
                    settle(v.scalar_tensor_tensor(out=qb[:, 0:16], in0=qa[:, 0:16],
                                                  scalar=-C2, in1=qb[:, 0:16],
                                                  op0=ALU.mult, op1=ALU.add))
                b.vector(f_q2rr)
                vd_q = cnt["vd"]

                def b_q2cos(s, vd_q=vd_q):
                    s.wait_ge(sem["vd"], vd_q)
                    s.activation(out=qc[:, 0:16], in_=qb[:, 0:16],
                                 func=AF.Sin, bias=g.halfpi[:, :]).then_inc(sem["sd"], 1)
                sd_q = sdefer(g, 1, b_q2cos, 'sin')

                def f_q2scan(v, sd_q=sd_q):
                    v.wait_ge(sem["sd"], sd_q)
                    inc(g, "vd", v.tensor_tensor_scan(
                        out=qd[:, 0:16], data0=qc[:, 0:16],
                        data1=g.zero64[:, 0:16], initial=1.0,
                        op0=ALU.mult, op1=ALU.add))
                b.vector(f_q2scan)
                vd_qs = cnt["vd"]

                def f_q2tr2(t, vd_qs=vd_qs):
                    t.wait_ge(sem["vd"], vd_qs)
                    inc(g, "td", t.transpose(g.psB[0:16, 1152:1280], qd[:, 0:16],
                                             g.eye32[:, :]))
                b.tensor(f_q2tr2)
                td_q2 = cnt["td"]

                def f_q2ev(v, ci=ci, lo=lo, td_q2=td_q2):
                    v.wait_ge(sem["td"], td_q2)
                    if ci >= 2 and lo == 0:
                        v.wait_ge(sem["td"], hmm_td[(ci - 2, 31)])
                    v.tensor_copy(out=qf[0:16, lo:lo + 128],
                                  in_=g.psB[0:16, 1152:1280])
                    inc(g, "vd", v.tensor_copy(out=qf[64:80, lo:lo + 128],
                                               in_=g.psB[0:16, 1152:1280]))
                b.vector(f_q2ev)
            vd_qf[ci] = cnt["vd"]

        def emit_hstage(ci):
            par = ci % 2
            qf = qf2[par]
            for mt in range(32):
                def f_hmm(t, ci=ci, mt=mt):
                    if mt == 0:
                        t.wait_ge(sem["vd"], vd_qf[ci])
                        t.wait_ge(sem["dg"], dg_w)
                    if mt >= 2:
                        t.wait_ge(sem["sd"], hev_sd[(ci, mt - 2)])
                    reg = g.psA[:, 1024 + (mt % 2) * 512:1024 + (mt % 2) * 512 + W]
                    hmm_td[(ci, mt)] = inc(g, "td", t.matmul(
                        reg,
                        w1t[(mt % 2) * 64:(mt % 2) * 64 + 16,
                            (mt // 2) * 128:(mt // 2) * 128 + 128],
                        qf[(mt % 2) * 64:(mt % 2) * 64 + 16, :],
                        start=True, stop=True))
                b.tensor(f_hmm)

                def b_hev(s, ci=ci, mt=mt):
                    s.wait_ge(sem["td"], hmm_td[(ci, mt)])
                    reg = g.psA[:, 1024 + (mt % 2) * 512:1024 + (mt % 2) * 512 + W]
                    s.activation(
                        out=hc[mt // 4][:, (mt % 4) * W:(mt % 4 + 1) * W],
                        in_=reg, func=AF.Relu,
                        bias=g.b1c[:, mt:mt + 1]).then_inc(sem["sd"], 1)
                hev_sd[(ci, mt)] = sdefer(g, 1, b_hev, 'c')

        def emit_tail(ci):
            par = ci % 2
            c0 = ci * W
            tc = tc2[par]
            x1c = x1c2[par]
            lns = lns2[par]
            for mto in range(PT):
                def f_fmm(t, ci=ci, mto=mto):
                    if mto == 0:
                        t.wait_ge(sem["sd"], hev_sd[(ci, 31)])
                    if mto >= 2:
                        t.wait_ge(sem["vd"], fev_vd[(ci, mto - 2)])
                    elif ci > 0:
                        t.wait_ge(sem["vd"], fev_vd[(ci - 1, 6 + mto)])
                    reg = g.psB[:, (mto % 2) * 512:(mto % 2) * 512 + W]
                    last = None
                    for kt in range(32):
                        last = t.matmul(
                            reg, w2t[:, kt * 1024 + mto * 128:kt * 1024 + (mto + 1) * 128],
                            hc[kt // 4][:, (kt % 4) * W:(kt % 4 + 1) * W],
                            start=(kt == 0), stop=(kt == 31))
                    fmm_td[(ci, mto)] = inc(g, "td", last)
                b.tensor(f_fmm)

                def f_fev(v, ci=ci, mto=mto):
                    v.wait_ge(sem["td"], fmm_td[(ci, mto)])
                    reg = g.psB[:, (mto % 2) * 512:(mto % 2) * 512 + W]
                    v.tensor_scalar(out=lns["lnT"][:, 0:W], in0=reg,
                                    scalar1=g.b2c[:, mto:mto + 1], scalar2=None,
                                    op0=ALU.add)
                    fev_vd[(ci, mto)] = inc(g, "vd", v.tensor_tensor(
                        out=tc[mto][:, :], in0=lns["lnT"][:, 0:W],
                        in1=x1c[mto][:, :], op=ALU.add))
                b.vector(f_fev)

            emit_ln(g, lns, [tc[pt][:, :] for pt in range(PT)],
                    [x1c[pt][:, :] for pt in range(PT)], W, pso=par * 512,
                    gamma=[g.g2c[:, pt:pt + 1] for pt in range(PT)],
                    beta=[g.bt2c[:, pt:pt + 1] for pt in range(PT)])
            vd_ln2 = cnt["vd"]

            def f_res(v, c0=c0, vd_ln2=vd_ln2, x1c=x1c):
                v.wait_ge(sem["vd"], vd_ln2)
                last = None
                for pt in range(PT):
                    last = v.tensor_tensor(out=g.xs[pt][:, c0:c0 + W],
                                           in0=g.xs[pt][:, c0:c0 + W],
                                           in1=x1c[pt][:, :], op=ALU.add)
                inc(g, "vd", last)
            b.vector(f_res)

        for ci in range(8):
            emit_front(ci)
            if ci > 0:
                emit_tail(ci - 1)
            emit_hstage(ci)
        emit_tail(7)
        join(g)


def emit_l1ffn(g):
    b, sem, cnt, nc = g.block, g.sem, g.cnt, g.nc
    g.tag += 1
    tg = g.tag
    with ExitStack() as ph:
        g.wT8 = ph.enter_context(nc.sbuf_tensor(f"ph{tg}_wT8", [8, 2048], BF16))
        sbc = lambda nm, sh, dt=F32: ph.enter_context(nc.sbuf_tensor(f"ph{tg}_{nm}", sh, dt))
        with ExitStack() as stA:
            t0 = [stA.enter_context(nc.sbuf_tensor(f"ph{tg}_t0_{i}", [128, T], BF16))
                  for i in range(PT)]
            lns = alloc_ln(g, stA, T, f"ph{tg}ln")
            emit_ln(g, lns, [g.xs[pt][:, :] for pt in range(PT)],
                    [t0[pt][:, :] for pt in range(PT)], T)
            emit_gating(g, t0, g.l1gw_sb, "l1")
            vd_g = cnt["vd"]

            def f_spill(q):
                q.wait_ge(sem["vd"], vd_g)
                for pt in range(PT):
                    inc(g, "ds", q.dma_start(out=g.t0_dram[pt * 128:(pt + 1) * 128, :],
                                             in_=t0[pt][:, :]), 16)
            b.sync(f_spill)
            ds_spill = cnt["ds"]
            join(g)

        wsl = [sbc(f"wsl{i}", [128, 11264], BF16) for i in range(4)]
        tch = [[sbc(f"tch{bi}_{i}", [128, 512], BF16) for i in range(PT)]
               for bi in range(2)]
        t0e = [sbc(f"t0e_{i}", [128, 512], BF16) for i in range(PT)]
        hshared = [sbc(f"hb_{i}", [128, 512], BF16) for i in range(2, 11)]
        hdup = [[sbc(f"hd{bi}_{i}", [128, 512], BF16) for i in range(2)]
                for bi in range(2)]

        def hbuf_of(idx, mc):
            return hdup[idx % 2][mc] if mc < 2 else hshared[mc - 2]
        scS = [sbc(f"scS{i}", [128, 512], BF16) for i in range(2)]

        roles = []
        cur = (0, 1, 2, 3)
        for e in range(8):
            roles.append(cur)
            cur = (cur[0], cur[3], cur[1], cur[2])

        WG, WU, WD = g.l1f_wg, g.l1f_wu, g.l1f_wd

        def slab_dma(q, dst, WSRC, e, nk):
            return q.dma_start(out=dst[:, :], in_=WSRC[e * 128:(e + 1) * 128, :])

        dgw = {}

        def f_w0(q):
            dgw[(0, "wg")] = inc(g, "dg", slab_dma(q, wsl[0], WG, 0, 8), 16)
            dgw[(0, "wu")] = inc(g, "dg", slab_dma(q, wsl[1], WU, 0, 8), 16)
            dgw[(0, "wd")] = inc(g, "dg", slab_dma(q, wsl[2], WD, 0, 11), 16)
        b.gpsimd(f_w0)

        def emit_wd(idx):
            e_, n_ = divmod(idx, 4)
            sd_w = roles[e_][2]
            for mt in range(PT):
                def f_wd(t, e=e_, n=n_, mt=mt, sd_=sd_w, idx=idx):
                    gi = idx * 8 + mt
                    slot = gi % 3
                    reg = g.psB[:, slot * 512:(slot + 1) * 512]
                    if mt == 0:
                        t.wait_ge(sem["dg"], dgw[(e, "wd")])
                        t.wait_ge(sem["vd"], hvd[(e, n, 10)])
                    if gi >= 3:
                        t.wait_ge(sem["vd"], evd[gi - 3])
                    last = None
                    for mc in range(11):
                        last = t.matmul(
                            reg,
                            wsl[sd_][:, mc * 1024 + mt * 128:mc * 1024 + (mt + 1) * 128],
                            hbuf_of(idx, mc)[:, :],
                            start=(mc == 0), stop=(mc == 10))
                    wdtd[(e, n, mt, "d")] = inc(g, "td", last)
                    if mt == 7:
                        wddone[(e, n)] = cnt["td"]
                b.tensor(f_wd)

                def f_ev(v, e=e_, n=n_, mt=mt, idx=idx):
                    gi = idx * 8 + mt
                    slot = gi % 3
                    v.wait_ge(sem["td"], wdtd[(e, n, mt, "d")])
                    evd[gi] = inc(g, "vd", v.scalar_tensor_tensor(
                        out=g.xs[mt][:, n * 512:(n + 1) * 512],
                        in0=g.psB[:, slot * 512:(slot + 1) * 512],
                        scalar=0.5,
                        in1=g.xs[mt][:, n * 512:(n + 1) * 512],
                        op0=ALU.mult, op1=ALU.add))
                b.vector(f_ev)

        gdone = {}
        udone = {}
        wddone = {}
        gchunk_done = {}
        bct = {}
        scale_vd = {}
        scale0 = {}
        hvd = {}
        sdsil = {}
        evd = {}
        wdtd = {}
        tchds = {}

        for e in range(8):
            sg, su, sd_, sfree = roles[e]

            for n in range(4):
                bi = (e * 4 + n) % 2

                def f_tch(q, e=e, n=n, bi=bi):
                    if e == 0 and n == 0:
                        q.wait_ge(sem["ds"], ds_spill)
                    idx = e * 4 + n
                    if idx >= 2:
                        pe, pn = divmod(idx - 2, 4)
                        q.wait_ge(sem["td"], gchunk_done[(pe, pn)])
                    for pt in range(PT):
                        tchds[(e, n)] = inc(g, "ds", q.dma_start(
                            out=tch[bi][pt][:, :],
                            in_=g.t0_dram[pt * 128:(pt + 1) * 128,
                                          n * 512:(n + 1) * 512]), 16)
                b.sync(f_tch)

                def f_bc(t, e=e, n=n):
                    if e == 0 and n == 0:
                        t.wait_ge(sem["sd"], g.sd_wT8)
                    if e > 0 or n > 0:
                        pe, pn = (e, n - 1) if n > 0 else (e - 1, 3)
                        t.wait_ge(sem["vd"], scale_vd[(pe, pn)])
                    bct[(e, n)] = inc(g, "td", t.matmul(
                        g.psB[:, 1536:2048],
                        g.oh8_bf[:, e * 128:(e + 1) * 128],
                        g.wT8[:, n * 512:(n + 1) * 512],
                        start=True, stop=True))
                b.tensor(f_bc)

                scale0[(e, n)] = cnt["vd"]

                def f_scale(v, e=e, n=n, bi=bi):
                    v.wait_ge(sem["td"], bct[(e, n)])
                    v.wait_ge(sem["ds"], tchds[(e, n)])
                    if e > 0 or n > 0:
                        pe, pn = (e, n - 1) if n > 0 else (e - 1, 3)
                        v.wait_ge(sem["td"], udone[(pe, pn)])
                    for kt in range(PT):
                        inc(g, "vd", v.tensor_tensor(out=t0e[kt][:, :],
                                                     in0=tch[bi][kt][:, :],
                                                     in1=g.psB[:, 1536:2048],
                                                     op=ALU.mult))
                    scale_vd[(e, n)] = cnt["vd"]
                b.vector(f_scale)

                idx_here = e * 4 + n
                if idx_here >= 1:
                    emit_wd(idx_here - 1)
                for mc in range(11):
                    def f_gmm(t, e=e, n=n, mc=mc, bi=bi, sg=sg):
                        if mc == 0:
                            t.wait_ge(sem["dg"], dgw[(e, "wg")])
                            t.wait_ge(sem["ds"], tchds[(e, n)])
                        idx = e * 4 + n
                        if mc >= 2:
                            t.wait_ge(sem["sd"], sdsil[(e, n, mc - 2)])
                        elif idx > 0:
                            pe_, pn_ = divmod(idx - 1, 4)
                            t.wait_ge(sem["sd"], sdsil[(pe_, pn_, 10 - mc)])
                        reg = g.psA[:, (mc % 2) * 512:(mc % 2) * 512 + 512]
                        last = None
                        for kt in range(PT):
                            last = t.matmul(
                                reg,
                                wsl[sg][:, kt * 1408 + mc * 128:kt * 1408 + (mc + 1) * 128],
                                tch[bi][kt][:, :],
                                start=(kt == 0), stop=(kt == 7))
                        wdtd[(e, n, mc, "g")] = inc(g, "td", last)
                        if mc == 10:
                            gchunk_done[(e, n)] = cnt["td"]
                            if n == 3:
                                gdone[e] = cnt["td"]
                    b.tensor(f_gmm)

                    def f_umm(t, e=e, n=n, mc=mc, su=su):
                        if mc == 0:
                            t.wait_ge(sem["dg"], dgw[(e, "wu")])
                        idx = e * 4 + n
                        if mc >= 2:
                            t.wait_ge(sem["vd"], hvd[(e, n, mc - 2)])
                        elif idx > 0:
                            pe_, pn_ = divmod(idx - 1, 4)
                            t.wait_ge(sem["vd"], hvd[(pe_, pn_, 10 - mc)])
                        reg = g.psA[:, 1024 + (mc % 2) * 512:1024 + (mc % 2) * 512 + 512]
                        last = None
                        for kt in range(PT):
                            if mc == 0:
                                t.wait_ge(sem["vd"], scale0[(e, n)] + kt + 1)
                            last = t.matmul(
                                reg,
                                wsl[su][:, kt * 1408 + mc * 128:kt * 1408 + (mc + 1) * 128],
                                t0e[kt][:, :],
                                start=(kt == 0), stop=(kt == 7))
                        wdtd[(e, n, mc, "u")] = inc(g, "td", last)
                        if mc == 10 and n == 3:
                            udone2 = cnt["td"]
                        if mc == 10:
                            udone[(e, n)] = cnt["td"]
                    b.tensor(f_umm)

                    def b_sil(s, e=e, n=n, mc=mc):
                        s.wait_ge(sem["td"], wdtd[(e, n, mc, "g")])
                        idx = e * 4 + n
                        if mc >= 2:
                            s.wait_ge(sem["vd"], hvd[(e, n, mc - 2)])
                        elif idx > 0:
                            pe_, pn_ = divmod(idx - 1, 4)
                            s.wait_ge(sem["vd"], hvd[(pe_, pn_, 10 - mc)])
                        reg = g.psA[:, (mc % 2) * 512:(mc % 2) * 512 + 512]
                        s.activation(out=scS[mc % 2][:, :], in_=reg,
                                     func=AF.Silu).then_inc(sem["sd"], 1)
                    sdsil[(e, n, mc)] = sdefer(g, 1, b_sil, 'silu')

                    def f_h(v, e=e, n=n, mc=mc):
                        v.wait_ge(sem["td"], wdtd[(e, n, mc, "u")])
                        v.wait_ge(sem["sd"], sdsil[(e, n, mc)])
                        idx = e * 4 + n
                        if idx >= 2 and mc < 2:
                            pe_, pn_ = divmod(idx - 2, 4)
                            v.wait_ge(sem["td"], wdtd[(pe_, pn_, 7, "d")])
                        reg = g.psA[:, 1024 + (mc % 2) * 512:1024 + (mc % 2) * 512 + 512]
                        idx = e * 4 + n
                        hvd[(e, n, mc)] = inc(g, "vd", v.tensor_tensor(
                            out=hbuf_of(idx, mc)[:, :], in0=reg,
                            in1=scS[mc % 2][:, :], op=ALU.mult))
                    b.vector(f_h)


            if e == 7:
                emit_wd(31)
            if e < 7:
                def f_pref(q, e=e):
                    nsg, nsu, nsd, nfree = roles[e + 1]
                    if e >= 1:
                        q.wait_ge(sem["td"], wddone[(e - 1, 3)])
                    dgw[(e + 1, "wu")] = inc(g, "dg", slab_dma(q, wsl[nsu], WU, e + 1, 8), 16)
                    q.wait_ge(sem["td"], gdone[e])
                    dgw[(e + 1, "wg")] = inc(g, "dg", slab_dma(q, wsl[nsg], WG, e + 1, 8), 16)
                    q.wait_ge(sem["td"], udone[(e, 3)])
                    dgw[(e + 1, "wd")] = inc(g, "dg", slab_dma(q, wsl[nsd], WD, e + 1, 11), 16)
                b.gpsimd(f_pref)
        join(g)



def emit_l1attn(g):
    b, sem, cnt, nc = g.block, g.sem, g.cnt, g.nc
    g.tag += 1
    tg = g.tag
    with ExitStack() as ph:
        sbc = lambda nm, sh, dt=F32: ph.enter_context(nc.sbuf_tensor(f"ph{tg}_{nm}", sh, dt))
        tt0 = [sbc(f"t_{i}", [128, T], BF16) for i in range(PT)]
        pk = sbc("pk", [128, 24])
        halo = sbc("halo", [128, 24])
        packS = sbc("packS", [128, 16])
        gath = sbc("gath", [128, 128])
        carryc = sbc("carryc", [128, 8])
        cfold = sbc("cfold", [128, 8])
        colt = sbc("colt", [128, 1])

        with ExitStack() as stA:
            lns = alloc_ln(g, stA, T, f"ph{tg}ln")
            emit_ln(g, lns, [g.xs[pt][:, :] for pt in range(PT)],
                    [tt0[pt][:, :] for pt in range(PT)], T)
            join(g)
        x1bf = [sbc(f"x1_{i}", [128, 2056], BF16) for i in range(PT)]

        stBC = ExitStack()
        gi_sl = stBC.enter_context(nc.sbuf_tensor(f"ph{tg}_gisl", [128, 8192], BF16))
        ga_sl = stBC.enter_context(nc.sbuf_tensor(f"ph{tg}_gasl", [128, 8192], BF16))
        with ExitStack() as stB:
            wx = stB.enter_context(nc.sbuf_tensor(f"ph{tg}_wx", [128, 8192], BF16))

            def f_wx(q):
                inc(g, "dg", q.dma_start(out=wx[:, :], in_=g.l1a_wx), 16)
                for dst, W_ in ((gi_sl, g.l1a_gi), (ga_sl, g.l1a_ga)):
                    inc(g, "dg", q.dma_start(out=dst[:, :], in_=W_), 16)
            b.gpsimd(f_wx)
            dg_wx = cnt["dg"]

            x1td = {}
            x1vd = {}
            for ni, n in enumerate([3, 0, 1, 2]):
                for mt in range(PT):
                    def f_mm(t, n=n, mt=mt, ni=ni):
                        gi = ni * 8 + mt
                        if gi == 0:
                            t.wait_ge(sem["dg"], dg_wx)
                        slot = gi % 4
                        reg = g.psB[:, slot * 512:(slot + 1) * 512]
                        if gi >= 4:
                            t.wait_ge(sem["vd"], x1vd[gi - 4])
                        last = None
                        for kt in range(PT):
                            last = t.matmul(
                                reg, wx[:, kt * 1024 + mt * 128:kt * 1024 + (mt + 1) * 128],
                                tt0[kt][:, n * 512:(n + 1) * 512],
                                start=(kt == 0), stop=(kt == 7))
                        x1td[gi] = inc(g, "td", last)
                    b.tensor(f_mm)

                    def f_ev(v, n=n, mt=mt, ni=ni):
                        gi = ni * 8 + mt
                        slot = gi % 4
                        v.wait_ge(sem["td"], x1td[gi])
                        x1vd[gi] = inc(g, "vd", v.tensor_scalar(
                            out=x1bf[mt][:, 3 + n * 512:3 + (n + 1) * 512],
                            in0=g.psB[:, slot * 512:(slot + 1) * 512],
                            scalar1=g.bxc[:, mt:mt + 1], scalar2=None,
                            op0=ALU.add))
                    b.vector(f_ev)
                if ni == 0:
                    def f_pub(v):
                        last = None
                        for pt in range(PT):
                            last = v.tensor_copy(out=pk[:, pt * 3:(pt + 1) * 3],
                                                 in_=x1bf[pt][:, 2048:2051])
                        inc(g, "vd", last)
                    b.vector(f_pub)
                    vd_pub = cnt["vd"]

                    def f_cc(q):
                        q.wait_ge(sem["vd"], vd_pub)
                        inc(g, "dg", q.dma_start(out=g.cc1_in[:, :], in_=pk[:, :]), 16)
                        q.wait_ge(sem["dg"], cnt["dg"])
                        inc(g, "cc", q.collective_compute(
                            "AllGather", ALU.bypass,
                            replica_groups=[list(range(NCORES))],
                            ins=[g.cc1_in.ap().opt()],
                            outs=[g.cc1_out.ap().opt()]))
                    b.gpsimd(f_cc)
                    cc_1 = cnt["cc"]

                    def f_halo(q):
                        with q.register("rh") as rh:
                            q.wait_ge(sem["cc"], cc_1)
                            q.reg_load(rh, g.prev_in)
                            q.reg_mul(rh, rh, 24 * 128)
                            inc(g, "ds", q.dma_start(
                                out=halo[:, :],
                                in_=bass.AP(g.cc1_out, rh, [[24, 128], [1, 24]])), 16)
                    b.sync(f_halo)
                    ds_h = cnt["ds"]

                    def f_mask(v):
                        def settle(ins):
                            v.wait_ge(sem["vd"], inc(g, "vd", ins))
                        v.wait_ge(sem["ds"], ds_h)
                        settle(v.tensor_scalar(out=halo[:, :], in0=halo[:, :],
                                               scalar1=g.mcol[:, :],
                                               scalar2=None, op0=ALU.mult))
                        for pt in range(PT):
                            settle(v.tensor_copy(out=x1bf[pt][:, 0:3],
                                                 in_=halo[:, pt * 3:(pt + 1) * 3]))
                    b.vector(f_mask)
            dg_halo = cnt["dg"]
            vd_halo = cnt["vd"]
            join(g)

        with ExitStack() as stC:
            sc2 = lambda nm, sh, dt=F32: stC.enter_context(nc.sbuf_tensor(f"ph{tg}_{nm}", sh, dt))
            x2b = [[sc2(f"x2b{bi}_{i}", [128, 256], BF16) for i in range(PT)]
                   for bi in range(2)]
            x2f1 = sc2("x2f1", [128, 256])
            x2f = [x2f1 for _ in range(PT)]
            gx2 = [sc2(f"gx{i}", [128, 256], BF16) for i in range(2)]
            sgF = [sc2(f"sg{i}", [128, 256], BF16) for i in range(2)]
            mlt = [sc2(f"ml{i}", [128, 256], BF16) for i in range(2)]
            lac = [sc2(f"lac{i}", [128, 256], BF16) for i in range(2)]
            bc2 = [sc2(f"bc2{i}", [128, 256], BF16) for i in range(2)]

            dg_g = cnt["dg"]

            cvd = {}
            gtd = {}
            atd = {}
            gsd = {}
            asd = {}
            bvd = {}
            lads = {}
            chunk_order = [1, 2, 3, 4, 5, 6, 7, 0]
            pos_of = {ci: p for p, ci in enumerate(chunk_order)}

            def emit_conv(ci):
                c0 = ci * 256

                def f_conv(v, ci=ci, c0=c0):
                    pos = pos_of[ci]
                    if pos == 0:
                        v.wait_ge(sem["vd"], cnt["vd"])
                    if ci == 0:
                        v.wait_ge(sem["vd"], vd_halo)
                    if pos >= 2:
                        v.wait_ge(sem["td"], atd[(pos - 2) * 8 + 7])
                    last = None
                    for pt in range(PT):
                        v.tensor_scalar(out=x2f[pt][:, :],
                                        in0=x1bf[pt][:, c0:c0 + 256],
                                        scalar1=g.cwc[:, pt:pt + 1],
                                        scalar2=g.cbc[:, pt:pt + 1],
                                        op0=ALU.mult, op1=ALU.add)
                        for i in (1, 2):
                            v.scalar_tensor_tensor(
                                out=x2f[pt][:, :],
                                in0=x1bf[pt][:, c0 + i:c0 + i + 256],
                                scalar=g.cwc[:, i * 8 + pt:i * 8 + pt + 1],
                                in1=x2f[pt][:, :], op0=ALU.mult, op1=ALU.add)
                        last = v.scalar_tensor_tensor(
                            out=x2b[pos_of[ci] % 2][pt][:, :],
                            in0=x1bf[pt][:, c0 + 3:c0 + 3 + 256],
                            scalar=g.cwc[:, 24 + pt:24 + pt + 1],
                            in1=x2f[pt][:, :], op0=ALU.mult, op1=ALU.add)
                    cvd[ci] = inc(g, "vd", last)
                b.vector(f_conv)

            emit_conv(chunk_order[0])
            for pos, ci in enumerate(chunk_order):
                c0 = ci * 256
                if pos + 1 < 8:
                    emit_conv(chunk_order[pos + 1])
                def emit_pA(mt, ci=ci, c0=c0, pos=pos):
                        def f_gmm(t, ci=ci, mt=mt, c0=c0, pos=pos):
                            gi = pos * 8 + mt
                            if gi == 0:
                                t.wait_ge(sem["dg"], dg_g)
                            if mt == 0:
                                t.wait_ge(sem["vd"], cvd[ci])
                            if gi >= 2:
                                t.wait_ge(sem["sd"], gsd[(gi - 2, "gx")])
                            reg = g.psA[:, (gi % 2) * 512:(gi % 2) * 512 + 256]
                            last = None
                            for kt in range(PT):
                                last = t.matmul(
                                    reg, gi_sl[:, kt * 1024 + mt * 128:kt * 1024 + (mt + 1) * 128],
                                    x2b[pos % 2][kt][:, :], start=(kt == 0), stop=(kt == 7))
                            gtd[gi] = inc(g, "td", last)
                        b.tensor(f_gmm)

                        def b_gev(s, ci=ci, mt=mt, pos=pos):
                            gi = pos * 8 + mt
                            s.wait_ge(sem["td"], gtd[gi])
                            if gi >= 2:
                                s.wait_ge(sem["vd"], bvd.get(gi - 2, 0))
                            reg = g.psA[:, (gi % 2) * 512:(gi % 2) * 512 + 256]
                            s.activation(
                                out=gx2[gi % 2][:, :], in_=reg, func=AF.Sigmoid,
                                bias=g.gibc[:, mt:mt + 1]).then_inc(sem["sd"], 1)
                        gsd[(pos * 8 + mt, "gx")] = sdefer(g, 1, b_gev, 'sig')

                        def f_amm(t, ci=ci, mt=mt, c0=c0, pos=pos):
                            gi = pos * 8 + mt
                            if gi >= 2:
                                t.wait_ge(sem["sd"], asd[(gi - 2, "la")])
                            reg = g.psA[:, 1024 + (gi % 2) * 512:1024 + (gi % 2) * 512 + 256]
                            last = None
                            for kt in range(PT):
                                last = t.matmul(
                                    reg, ga_sl[:, kt * 1024 + mt * 128:kt * 1024 + (mt + 1) * 128],
                                    x2b[pos % 2][kt][:, :], start=(kt == 0), stop=(kt == 7))
                            atd[gi] = inc(g, "td", last)
                        b.tensor(f_amm)

                        def b_aev(s, ci=ci, mt=mt, pos=pos):
                            gi = pos * 8 + mt
                            s.wait_ge(sem["td"], atd[gi])
                            if gi >= 2:
                                s.wait_ge(sem["vd"], bvd[(gi - 2, "lac")])
                            reg = g.psA[:, 1024 + (gi % 2) * 512:1024 + (gi % 2) * 512 + 256]
                            s.activation(
                                out=sgF[gi % 2][:, :], in_=reg, func=AF.Sigmoid,
                                bias=g.gabc[:, mt:mt + 1]).then_inc(sem["sd"], 1)
                        asd[(pos * 8 + mt, "sg")] = sdefer(g, 1, b_aev, 'sig')

                        def f_la(v, ci=ci, mt=mt, pos=pos):
                            gi = pos * 8 + mt
                            v.wait_ge(sem["sd"], asd[(gi, "sg")])
                            if gi >= 2:
                                v.wait_ge(sem["ds"], lads[gi - 2])
                                v.wait_ge(sem["sd"], asd[(gi - 2, "la")])
                            bvd[(gi, "lac")] = inc(g, "vd", v.tensor_scalar(
                                out=lac[gi % 2][:, :], in0=sgF[gi % 2][:, :],
                                scalar1=g.m8sp[:, mt:mt + 1], scalar2=None,
                                op0=ALU.mult))
                        b.vector(f_la)

                        def f_ladma(q, ci=ci, mt=mt, c0=c0, pos=pos):
                            gi = pos * 8 + mt
                            q.wait_ge(sem["vd"], bvd[(gi, "lac")])
                            lads[gi] = inc(g, "ds", q.dma_start(
                                out=g.la_dram[mt * 128:(mt + 1) * 128, c0:c0 + 256],
                                in_=lac[gi % 2][:, :]), 16)
                        b.sync(f_ladma)
                def emit_pB(mt, ci=ci, pos=pos):
                        def b_mlt(s, ci=ci, mt=mt, pos=pos):
                            gi = pos * 8 + mt
                            s.wait_ge(sem["vd"], bvd[(gi, "lac")])
                            if gi >= 2:
                                s.wait_ge(sem["vd"], bvd[gi - 2])
                            s.activation(
                                out=mlt[gi % 2][:, :], in_=lac[gi % 2][:, :],
                                func=AF.Exp, scale=2.0).then_inc(sem["sd"], 1)
                        asd[(pos * 8 + mt, "la")] = sdefer(g, 1, b_mlt, 'exp')

                        def f_mlt2(v, ci=ci, mt=mt, pos=pos):
                            gi = pos * 8 + mt
                            v.wait_ge(sem["sd"], asd[(gi, "la")])
                            v.tensor_scalar(out=mlt[gi % 2][:, :], in0=mlt[gi % 2][:, :],
                                            scalar1=-1.0, scalar2=1.0,
                                            op0=ALU.mult, op1=ALU.add)
                            inc(g, "vd", v.tensor_scalar(out=mlt[gi % 2][:, :],
                                                         in0=mlt[gi % 2][:, :],
                                                         scalar1=0.0, scalar2=None,
                                                         op0=ALU.max))
                        b.vector(f_mlt2)
                        return cnt["vd"]
                def emit_pC(mt, vd_m2, ci=ci, c0=c0, pos=pos):
                        def b_mlt3(s, ci=ci, mt=mt, vd_m2=vd_m2, pos=pos):
                            gi = pos * 8 + mt
                            s.wait_ge(sem["vd"], vd_m2)
                            s.activation(
                                out=mlt[gi % 2][:, :], in_=mlt[gi % 2][:, :],
                                func=AF.Sqrt).then_inc(sem["sd"], 1)
                        asd[(pos * 8 + mt, "sqrt")] = sdefer(g, 1, b_mlt3, 'sqrt')

                        def f_b(v, ci=ci, mt=mt, c0=c0, pos=pos):
                            gi = pos * 8 + mt
                            v.wait_ge(sem["sd"], asd[(gi, "sqrt")])
                            v.wait_ge(sem["sd"], gsd[(gi, "gx")])
                            if ci == 0:
                                v.wait_ge(sem["vd"], inc(g, "vd", v.tensor_scalar(
                                    out=colt[:, :], in0=mlt[gi % 2][:, 0:1],
                                    scalar1=g.mcol[:, :], scalar2=None, op0=ALU.mult)))
                                v.wait_ge(sem["vd"], inc(g, "vd", v.tensor_tensor(
                                    out=mlt[gi % 2][:, 0:1], in0=colt[:, :],
                                    in1=g.m1col[:, :], op=ALU.add)))
                            if gi >= 2:
                                v.wait_ge(sem["ds"], lads[(gi - 2, "b")])
                            v.tensor_tensor(out=bc2[gi % 2][:, :], in0=gx2[gi % 2][:, :],
                                            in1=x2b[pos % 2][mt][:, :], op=ALU.mult)
                            last = v.tensor_tensor(out=bc2[gi % 2][:, :],
                                                   in0=bc2[gi % 2][:, :],
                                                   in1=mlt[gi % 2][:, :], op=ALU.mult)
                            bvd[gi] = inc(g, "vd", last)
                        b.vector(f_b)

                        def f_bdma(q, ci=ci, mt=mt, c0=c0, pos=pos):
                            gi = pos * 8 + mt
                            q.wait_ge(sem["vd"], bvd[gi])
                            lads[(gi, "b")] = inc(g, "ds", q.dma_start(
                                out=g.b_dram[mt * 128:(mt + 1) * 128, c0:c0 + 256],
                                in_=bc2[gi % 2][:, :]), 16)
                        b.sync(f_bdma)
                for mt0 in (0, 2, 4, 6):
                    for mt in (mt0, mt0 + 1):
                        emit_pA(mt)
                    m2 = {}
                    for mt in (mt0, mt0 + 1):
                        m2[mt] = emit_pB(mt)
                    for mt in (mt0, mt0 + 1):
                        emit_pC(mt, m2[mt])
            join(g)
        stBC.close()

        with ExitStack() as stD:
            wy = stD.enter_context(nc.sbuf_tensor(f"ph{tg}_wy", [128, 8192], BF16))

            def f_wy(q):
                inc(g, "dg", q.dma_start(out=wy[:, :], in_=g.l1a_wy), 16)
            b.gpsimd(f_wy)
            dg_wy = cnt["dg"]
            larow = stD.enter_context(nc.sbuf_tensor(f"ph{tg}_lar", [128, T], BF16))
            brow = stD.enter_context(nc.sbuf_tensor(f"ph{tg}_brw", [128, T], BF16))
            af32 = stD.enter_context(nc.sbuf_tensor(f"ph{tg}_af", [128, T], F32))

            yfull = [stD.enter_context(nc.sbuf_tensor(f"ph{tg}_yf{i}", [128, T], BF16))
                     for i in range(PT)]
            hyb = stD.enter_context(nc.sbuf_tensor(f"ph{tg}_hy", [128, 512], BF16))
            ytd = {}
            ysd = {}
            yvd = {}
            scan_vd = {}
            lrds = {}
            for p_ass in (1, 2):
                for pt in range(PT):
                    prev = None
                    if (p_ass, pt) != (1, 0):
                        prev = (p_ass, pt - 1) if pt > 0 else (p_ass - 1, 7)

                    def f_lrd(q, pt=pt, p_ass=p_ass, prev=prev):
                        if prev is not None:
                            q.wait_ge(sem["sd"], asd[("exp",) + prev])
                            q.wait_ge(sem["vd"], scan_vd[prev])
                        inc(g, "ds", q.dma_start(
                            out=larow[:, :],
                            in_=g.la_dram[pt * 128:(pt + 1) * 128, :]), 16)
                        lrds[(p_ass, pt)] = inc(g, "ds", q.dma_start(
                            out=brow[:, :],
                            in_=g.b_dram[pt * 128:(pt + 1) * 128, :]), 16)
                    b.sync(f_lrd)

                    def b_exp(s, pt=pt, p_ass=p_ass, prev=prev):
                        s.wait_ge(sem["ds"], lrds[(p_ass, pt)])
                        if prev is not None:
                            s.wait_ge(sem["vd"], scan_vd[prev])
                        s.activation(out=af32[:, :], in_=larow[:, :],
                                     func=AF.Exp).then_inc(sem["sd"], 1)
                    asd[("exp", p_ass, pt)] = sdefer(g, 1, b_exp, 'exp')

                    def f_patch(v, pt=pt, p_ass=p_ass):
                        v.wait_ge(sem["sd"], asd[("exp", p_ass, pt)])
                        inc(g, "vd", v.tensor_scalar(
                            out=af32[:, 0:1], in0=af32[:, 0:1],
                            scalar1=g.mcol[:, :], scalar2=None, op0=ALU.mult))
                    b.vector(f_patch)
                    vd_p = cnt["vd"]

                    if p_ass == 1:
                        def f_scan1(v, pt=pt, vd_p=vd_p):
                            v.wait_ge(sem["vd"], vd_p)
                            v.tensor_tensor_scan(out=x1bf[pt][:, 0:T],
                                                 data0=af32[:, :],
                                                 data1=brow[:, :],
                                                 initial=0.0,
                                                 op0=ALU.mult, op1=ALU.add)
                            v.wait_ge(sem["vd"], inc(g, "vd", v.tensor_reduce(
                                out=colt[:, :], in_=larow[:, :], axis=AX,
                                op=ALU.add)))
                            scan_vd[(1, pt)] = inc(g, "vd", v.tensor_copy(
                                out=packS[:, 2 * pt + 1:2 * pt + 2],
                                in_=x1bf[pt][:, T - 1:T]))
                        b.vector(f_scan1)
                        svd = cnt["vd"]

                        def b_Ak(s, pt=pt, svd=svd):
                            s.wait_ge(sem["vd"], svd)
                            s.activation(
                                out=packS[:, 2 * pt:2 * pt + 1], in_=colt[:, :],
                                func=AF.Exp).then_inc(sem["sd"], 1)
                        asd[("Ak", pt)] = sdefer(g, 1, b_Ak, 'exp')

                        for n in range(4):
                            yi_ = pt * 4 + n
                            def f_ymm(t, n=n, pt=pt):
                                yi = pt * 4 + n
                                if yi == 0:
                                    t.wait_ge(sem["dg"], dg_wy)
                                    t.wait_ge(sem["vd"], cnt["vd"])
                                if yi >= 2:
                                    t.wait_ge(sem["sd"], ysd[yi - 2])
                                reg = g.psA[:, (yi % 2) * 512:(yi % 2) * 512 + 512]
                                last = None
                                for kt in range(PT):
                                    last = t.matmul(
                                        reg, wy[:, kt * 1024 + pt * 128:kt * 1024 + (pt + 1) * 128],
                                        tt0[kt][:, n * 512:(n + 1) * 512],
                                        start=(kt == 0), stop=(kt == 7))
                                ytd[yi] = inc(g, "td", last)
                            b.tensor(f_ymm)

                            def b_yev(s, n=n, pt=pt):
                                yi = pt * 4 + n
                                s.wait_ge(sem["td"], ytd[yi])
                                reg = g.psA[:, (yi % 2) * 512:(yi % 2) * 512 + 512]
                                s.activation(
                                    out=yfull[pt][:, n * 512:(n + 1) * 512],
                                    in_=reg, func=AF.Gelu,
                                    bias=g.byc[:, pt:pt + 1]).then_inc(sem["sd"], 1)
                            ysd[yi_] = sdefer(g, 1, b_yev, 'gelu')
                    else:
                        def f_scan2(v, pt=pt, vd_p=vd_p):
                            v.wait_ge(sem["vd"], vd_p)
                            scan_vd[(2, pt)] = inc(g, "vd", v.tensor_tensor_scan(
                                out=x1bf[pt][:, 0:T], data0=af32[:, :],
                                data1=brow[:, :],
                                initial=carryc[:, pt:pt + 1],
                                op0=ALU.mult, op1=ALU.add))
                        b.vector(f_scan2)

                        for n in range(4):
                            def f_out(v, n=n, pt=pt):
                                yi = pt * 4 + n
                                v.wait_ge(sem["sd"], ysd[yi])
                                v.wait_ge(sem["vd"], scan_vd[(2, pt)])
                                v.tensor_tensor(out=hyb[:, :],
                                                in0=x1bf[pt][:, n * 512:(n + 1) * 512],
                                                in1=yfull[pt][:, n * 512:(n + 1) * 512],
                                                op=ALU.mult)
                                yvd[yi] = inc(g, "vd", v.tensor_tensor(
                                    out=g.xs[pt][:, n * 512:(n + 1) * 512],
                                    in0=g.xs[pt][:, n * 512:(n + 1) * 512],
                                    in1=hyb[:, :], op=ALU.add))
                            b.vector(f_out)

                if p_ass == 1:
                    sd_all = cnt["sd"]

                    def f_cc2(q):
                        q.wait_ge(sem["sd"], sd_all)
                        q.wait_ge(sem["vd"], cnt["vd"])
                        inc(g, "dg", q.dma_start(out=g.cc2_in[:, :], in_=packS[:, :]), 16)
                        q.wait_ge(sem["dg"], cnt["dg"])
                        inc(g, "cc", q.collective_compute(
                            "AllGather", ALU.bypass,
                            replica_groups=[list(range(NCORES))],
                            ins=[g.cc2_in.ap().opt()],
                            outs=[g.cc2_out.ap().opt()]))
                        q.wait_ge(sem["cc"], cnt["cc"])
                        for r in range(NCORES):
                            inc(g, "dg", q.dma_start(
                                out=gath[:, r * 16:(r + 1) * 16],
                                in_=g.cc2_out[r * 128:(r + 1) * 128, :]), 16)
                    b.gpsimd(f_cc2)
                    dg_cc2 = cnt["dg"]

                    def f_fold(v):
                        def settle(ins):
                            v.wait_ge(sem["vd"], inc(g, "vd", ins))
                        v.wait_ge(sem["dg"], dg_cc2)
                        for pt in range(PT):
                            settle(v.tensor_tensor_scan(
                                out=cfold[:, :],
                                data0=gath[:, 2 * pt::16],
                                data1=gath[:, 2 * pt + 1::16],
                                initial=0.0, op0=ALU.mult, op1=ALU.add))
                            settle(v.tensor_tensor(out=cfold[:, :], in0=cfold[:, :],
                                                   in1=g.onehot[:, :], op=ALU.mult))
                            settle(v.tensor_reduce(out=carryc[:, pt:pt + 1],
                                                   in_=cfold[:, :], axis=AX,
                                                   op=ALU.add))
                    b.vector(f_fold)
            join(g)

        join(g)


def build_program(phases):
    nc = bass.Bass("TRN2", target_bir_lowering=False)
    g = G()
    g.nc = nc
    g.tag = 0
    g.sc_pend = []

    ap = lambda t: t.ap()
    g.x_in = ap(nc.dram_tensor("x_in", [D, T], F32, kind="ExternalInput"))
    g.x_out = ap(nc.dram_tensor("x_out", [D, T], F32, kind="ExternalOutput"))
    g.l0f_w = ap(nc.dram_tensor("l0f_w", [8 * 128, 8192], F32, kind="ExternalInput"))
    g.l0f_b = ap(nc.dram_tensor("l0f_b", [8, 1024], F32, kind="ExternalInput"))
    g.l0f_gw = ap(nc.dram_tensor("l0f_gw", [128, 64], F32, kind="ExternalInput"))
    g.l0f_gb = ap(nc.dram_tensor("l0f_gb", [1, 8], F32, kind="ExternalInput"))
    g.eye_in = ap(nc.dram_tensor("eye", [128, 128], F32, kind="ExternalInput"))
    g.oh8_in = ap(nc.dram_tensor("oh8", [8, 1024], F32, kind="ExternalInput"))
    g.mcol_in = ap(nc.dram_tensor("mcol", [128, 1], F32, kind="ExternalInput"))
    g.oneh_in = ap(nc.dram_tensor("onehot", [128, 8], F32, kind="ExternalInput"))
    g.prev_in = ap(nc.dram_tensor("prev_idx", [1, 1], U32, kind="ExternalInput"))
    if DEBUG_TAPS:
        g.tap_t0 = ap(nc.dram_tensor("tap_t0", [D, T], BF16, kind="ExternalOutput"))
        g.tap_gw = ap(nc.dram_tensor("tap_gw", [128, 128], F32, kind="ExternalOutput"))
        g.tap_wT8 = ap(nc.dram_tensor("tap_wT8", [8, 2048], BF16, kind="ExternalOutput"))
        g.tap_lg = ap(nc.dram_tensor("tap_lg", [128, 128], F32, kind="ExternalOutput"))
        g.tap_nm = ap(nc.dram_tensor("tap_nm", [128, 16], F32, kind="ExternalOutput"))
        g.tap_ex = ap(nc.dram_tensor("tap_ex", [128, 128], F32, kind="ExternalOutput"))
    if phases >= 2:
        g.l0a_w1t = ap(nc.dram_tensor("l0a_w1t", [128, 2048], F32, kind="ExternalInput"))
        g.l0a_w2t = ap(nc.dram_tensor("l0a_w2t", [128, 32768], F32, kind="ExternalInput"))
        g.l0a_b1 = ap(nc.dram_tensor("l0a_b1", [128, 32], F32, kind="ExternalInput"))
        g.l0a_b2 = ap(nc.dram_tensor("l0a_b2", [128, 8], F32, kind="ExternalInput"))
        g.l0a_g1 = ap(nc.dram_tensor("l0a_g1", [128, 8], F32, kind="ExternalInput"))
        g.l0a_bt1 = ap(nc.dram_tensor("l0a_bt1", [128, 8], F32, kind="ExternalInput"))
        g.l0a_g2 = ap(nc.dram_tensor("l0a_g2", [128, 8], F32, kind="ExternalInput"))
        g.l0a_bt2 = ap(nc.dram_tensor("l0a_bt2", [128, 8], F32, kind="ExternalInput"))
    if phases >= 4:
        g.l1f_gw = ap(nc.dram_tensor("l1f_gw", [128, 64], F32, kind="ExternalInput"))
        g.l1f_wg = ap(nc.dram_tensor("l1f_wg", [8 * 128, 11264], F32, kind="ExternalInput"))
        g.l1f_wu = ap(nc.dram_tensor("l1f_wu", [8 * 128, 11264], F32, kind="ExternalInput"))
        g.l1f_wd = ap(nc.dram_tensor("l1f_wd", [8 * 128, 11264], F32, kind="ExternalInput"))
        g.t0_dram = ap(nc.dram_tensor("t0_dram", [D, T], BF16))
    if phases >= 5:
        g.l1a_wx = ap(nc.dram_tensor("l1a_wx", [128, 8192], F32, kind="ExternalInput"))
        g.l1a_wy = ap(nc.dram_tensor("l1a_wy", [128, 8192], F32, kind="ExternalInput"))
        g.l1a_gi = ap(nc.dram_tensor("l1a_gi", [128, 8192], F32, kind="ExternalInput"))
        g.l1a_ga = ap(nc.dram_tensor("l1a_ga", [128, 8192], F32, kind="ExternalInput"))
        g.l1a_bx = ap(nc.dram_tensor("l1a_bx", [128, 8], F32, kind="ExternalInput"))
        g.l1a_by = ap(nc.dram_tensor("l1a_by", [128, 8], F32, kind="ExternalInput"))
        g.l1a_gib = ap(nc.dram_tensor("l1a_gib", [128, 8], F32, kind="ExternalInput"))
        g.l1a_gab = ap(nc.dram_tensor("l1a_gab", [128, 8], F32, kind="ExternalInput"))
        g.l1a_cb = ap(nc.dram_tensor("l1a_cb", [128, 8], F32, kind="ExternalInput"))
        g.l1a_cw = ap(nc.dram_tensor("l1a_cw", [128, 32], F32, kind="ExternalInput"))
        g.l1a_ap = ap(nc.dram_tensor("l1a_ap", [128, 8], F32, kind="ExternalInput"))
        g.t_dram = ap(nc.dram_tensor("t_dram", [D, T], BF16))
        g.la_dram = ap(nc.dram_tensor("la_dram", [D, T], BF16))
        g.b_dram = ap(nc.dram_tensor("b_dram", [D, T], BF16))
        g.cc1_in = nc.dram_tensor("cc1_in", [128, 24], F32)
        g.cc1_out = nc.dram_tensor("cc1_out", [1024, 24], F32)
        g.cc2_in = nc.dram_tensor("cc2_in", [128, 16], F32)
        g.cc2_out = nc.dram_tensor("cc2_out", [1024, 16], F32)

    with ExitStack() as stack:
        g.sem = {}
        g.cnt = {}
        for name in ("ds", "dg", "td", "vd", "sd", "cc"):
            g.sem[name] = stack.enter_context(nc.semaphore(name))
            g.cnt[name] = 0
        sb = lambda nm, sh, dt=F32: stack.enter_context(nc.sbuf_tensor(nm, sh, dt))
        g.xs = [sb(f"x{i}", [128, T]) for i in range(PT)]
        g.psA = stack.enter_context(nc.psum_tensor("psA", [128, 2048], F32))
        g.psB = stack.enter_context(nc.psum_tensor("psB", [128, 2048], F32))
        g.eye32 = sb("eye32", [128, 128])
        g.ones_bf = sb("ones_bf", [128, 128], BF16)
        g.ones1_f32 = sb("ones1_f32", [1, 128])
        g.zero64 = sb("zero64", [128, 64])
        g.halfpi = sb("halfpi", [128, 1])
        g.mcol = sb("mcol_sb", [128, 1])
        g.m1col = sb("m1col_sb", [128, 1])
        g.onehot = sb("onehot_sb", [128, 8])
        g.gatw = sb("gatw", [128, 128])
        g.gsc = sb("gsc", [128, 128])
        g.gs2 = sb("gs2", [128, 128])
        g.gis1 = sb("gis1", [128, 128])
        g.gis2 = sb("gis2", [128, 128])
        g.gnm = sb("gnm", [128, 16])
        g.colA = sb("colA", [128, 1]); g.colB = sb("colB", [128, 1])
        g.colC = sb("colC", [128, 16]); g.colP = sb("colP", [128, 16])

        g.l0gw_sb = sb("l0gw_sb", [128, 64], BF16)
        g.oh8_bf = sb("oh8_bf", [8, 1024], BF16)
        g.gbrow_bf = sb("gbrow_bf", [1, 8], BF16)
        g.l0fb_bf = sb("l0fb_bf", [8, 1024], BF16)
        if phases >= 2:
            g.b1c = sb("b1c", [128, 32]); g.b2c = sb("b2c", [128, 8])
            g.g1c = sb("g1c", [128, 8]); g.bt1c = sb("bt1c", [128, 8])
            g.g2c = sb("g2c", [128, 8]); g.bt2c = sb("bt2c", [128, 8])
        if phases >= 4:
            g.l1gw_sb = sb("l1gw_sb", [128, 64], BF16)
        if phases >= 5:
            g.bxc = sb("bxc", [128, 8]); g.byc = sb("byc", [128, 8])
            g.gibc = sb("gibc", [128, 8]); g.gabc = sb("gabc", [128, 8])
            g.cbc = sb("cbc", [128, 8]); g.cwc = sb("cwc", [128, 32])
            g.apc = sb("apc", [128, 8]); g.m8sp = sb("m8sp", [128, 8])

        b = g.block = nc.Block()
        with b:
            def f_init_dma(q):
                for pt in range(PT):
                    inc(g, "ds", q.dma_start(out=g.xs[pt][:, :],
                                             in_=g.x_in[pt * 128:(pt + 1) * 128, :]), 16)
                inc(g, "ds", q.dma_start(out=g.eye32[:, :], in_=g.eye_in), 16)
                inc(g, "ds", q.dma_start(out=g.mcol[:, :], in_=g.mcol_in), 16)
                inc(g, "ds", q.dma_start(out=g.onehot[:, :], in_=g.oneh_in), 16)
                if phases >= 2:
                    for dst, src in [(g.b1c, g.l0a_b1), (g.b2c, g.l0a_b2),
                                     (g.g1c, g.l0a_g1), (g.bt1c, g.l0a_bt1),
                                     (g.g2c, g.l0a_g2), (g.bt2c, g.l0a_bt2)]:
                        inc(g, "ds", q.dma_start(out=dst[:, :], in_=src), 16)
                if phases >= 5:
                    for dst, src in [(g.bxc, g.l1a_bx), (g.byc, g.l1a_by),
                                     (g.gibc, g.l1a_gib), (g.gabc, g.l1a_gab),
                                     (g.cbc, g.l1a_cb), (g.cwc, g.l1a_cw),
                                     (g.apc, g.l1a_ap)]:
                        inc(g, "ds", q.dma_start(out=dst[:, :], in_=src), 16)
            b.sync(f_init_dma)

            def f_init_gdma(q):
                inc(g, "dg", q.dma_start(out=g.l0gw_sb[:, :], in_=g.l0f_gw), 16)
                inc(g, "dg", q.dma_start(out=g.oh8_bf[:, :], in_=g.oh8_in), 16)
                inc(g, "dg", q.dma_start(out=g.gbrow_bf[:, :], in_=g.l0f_gb), 16)
                inc(g, "dg", q.dma_start(out=g.l0fb_bf[:, :], in_=g.l0f_b), 16)
                if phases >= 4:
                    inc(g, "dg", q.dma_start(out=g.l1gw_sb[:, :], in_=g.l1f_gw), 16)
            b.gpsimd(f_init_gdma)

            ds_init = g.cnt["ds"]

            def f_init_v(v):
                v.memset(g.ones_bf[:, :], 1.0)
                v.memset(g.ones1_f32[:, :], 1.0)
                v.memset(g.zero64[:, :], 0.0)
                v.memset(g.halfpi[:, :], float(np.pi / 2))
                v.wait_ge(g.sem["ds"], ds_init)
                inc(g, "vd", v.tensor_scalar(out=g.m1col[:, :], in0=g.mcol[:, :],
                                             scalar1=-1.0, scalar2=1.0,
                                             op0=ALU.mult, op1=ALU.add))
            b.vector(f_init_v)

            if phases >= 5:
                vd_i = g.cnt["vd"]

                def f_sp(s):
                    s.wait_ge(g.sem["vd"], vd_i)
                    inc(g, "sd", s.activation(out=g.m8sp[:, :], in_=g.apc[:, :],
                                              func=AF.Exp))
                b.scalar(f_sp)
                sd_i = g.cnt["sd"]

                def f_sp2(v):
                    v.wait_ge(g.sem["sd"], sd_i)
                    inc(g, "vd", v.tensor_scalar(out=g.m8sp[:, :],
                                                 in0=g.m8sp[:, :], scalar1=1.0,
                                                 scalar2=None, op0=ALU.add))
                b.vector(f_sp2)
                vd_i2 = g.cnt["vd"]

                def f_sp3(s):
                    s.wait_ge(g.sem["vd"], vd_i2)
                    inc(g, "sd", s.activation(out=g.m8sp[:, :], in_=g.m8sp[:, :],
                                              func=AF.Ln))
                b.scalar(f_sp3)
                sd_i2 = g.cnt["sd"]

                def f_sp4(v):
                    v.wait_ge(g.sem["sd"], sd_i2)
                    inc(g, "vd", v.tensor_scalar(out=g.m8sp[:, :],
                                                 in0=g.m8sp[:, :], scalar1=-8.0,
                                                 scalar2=None, op0=ALU.mult))
                b.vector(f_sp4)

            join(g)

            seq = []
            if phases >= 1:
                seq.append(("P1_l0ffn", emit_l0ffn))
            if phases >= 2:
                seq.append(("P2_l0attn", emit_l0attn))
            if phases >= 3:
                seq.append(("P3_l0ffn", emit_l0ffn))
            if phases >= 4:
                seq.append(("P4_l1ffn", emit_l1ffn))
            if phases >= 5:
                seq.append(("P5_l1attn", emit_l1attn))
            if phases >= 6:
                seq.append(("P6_l1ffn", emit_l1ffn))
            for nm, fn in seq:
                with nc.named_scope(nm):
                    fn(g)

            def f_store(q):
                for pt in range(PT):
                    inc(g, "ds", q.dma_start(out=g.x_out[pt * 128:(pt + 1) * 128, :],
                                             in_=g.xs[pt][:, :]), 16)
                q.wait_ge(g.sem["ds"], g.cnt["ds"])
            b.sync(f_store)
    return g


def _colmajor(v):
    v = np.ascontiguousarray(np.asarray(v, dtype=np.float32))
    n = v.shape[0] // 128
    return np.ascontiguousarray(v.reshape(n, 128).T)


def _prep_shared(inputs, phases):
    W = {}
    W["eye"] = np.eye(128, dtype=np.float32)
    oh8 = np.zeros((8, 1024), np.float32)
    for e in range(8):
        oh8[e, e * 128:(e + 1) * 128] = 1.0
    W["oh8"] = oh8
    ew = np.asarray(inputs["L0_ffn_expert_w"], np.float32)
    W["l0f_w"] = np.ascontiguousarray(
        ew.reshape(8, 8, 128, 1024).transpose(0, 2, 1, 3).reshape(8 * 128, 8192))
    W["l0f_b"] = np.ascontiguousarray(np.asarray(inputs["L0_ffn_expert_b"], np.float32))
    gw = np.asarray(inputs["L0_ffn_gate_w"], np.float32)
    W["l0f_gw"] = np.ascontiguousarray(gw.reshape(8, 128, 8).transpose(1, 0, 2).reshape(128, 64))
    W["l0f_gb"] = np.ascontiguousarray(np.asarray(inputs["L0_ffn_gate_b"], np.float32).reshape(1, 8))
    if phases >= 2:
        w1T = np.asarray(inputs["L0_attn_w1"], np.float32).T
        w1p = np.zeros((128, 2048), np.float32)
        for mt in range(32):
            w1p[(mt % 2) * 64:(mt % 2) * 64 + 16,
                (mt // 2) * 128:(mt // 2) * 128 + 128] = w1T[:, mt * 128:(mt + 1) * 128]
        W["l0a_w1t"] = w1p
        w2t = np.asarray(inputs["L0_attn_w2"], np.float32).T
        W["l0a_w2t"] = np.ascontiguousarray(
            w2t.reshape(32, 128, 1024).transpose(1, 0, 2).reshape(128, 32768))
        W["l0a_b1"] = _colmajor(inputs["L0_attn_b1"])
        W["l0a_b2"] = _colmajor(inputs["L0_attn_b2"])
        W["l0a_g1"] = _colmajor(inputs["L0_attn_gamma1"])
        W["l0a_bt1"] = _colmajor(inputs["L0_attn_beta1"])
        W["l0a_g2"] = _colmajor(inputs["L0_attn_gamma2"])
        W["l0a_bt2"] = _colmajor(inputs["L0_attn_beta2"])
    def _slab(a, nk, m):
        a = np.asarray(a, np.float32).reshape(8, nk, 128, m)
        return np.ascontiguousarray(a.transpose(0, 2, 1, 3).reshape(8 * 128, nk * m))
    if phases >= 4:
        gw1 = np.asarray(inputs["L1_ffn_w_gate"], np.float32)
        W["l1f_gw"] = np.ascontiguousarray(gw1.reshape(8, 128, 8).transpose(1, 0, 2).reshape(128, 64))
        W["l1f_wg"] = _slab(inputs["L1_ffn_w_g"], 8, 1408)
        W["l1f_wu"] = _slab(inputs["L1_ffn_w_u"], 8, 1408)
        W["l1f_wd"] = _slab(inputs["L1_ffn_w_d"], 11, 1024)
    if phases >= 5:
        W["l1a_wx"] = np.ascontiguousarray(np.asarray(inputs["L1_attn_w_x"], np.float32).reshape(8, 128, 1024).transpose(1, 0, 2).reshape(128, 8192))
        W["l1a_wy"] = np.ascontiguousarray(np.asarray(inputs["L1_attn_w_y"], np.float32).reshape(8, 128, 1024).transpose(1, 0, 2).reshape(128, 8192))
        W["l1a_gi"] = np.ascontiguousarray(np.asarray(inputs["L1_attn_gi_w"], np.float32).reshape(8, 128, 1024).transpose(1, 0, 2).reshape(128, 8192))
        W["l1a_ga"] = np.ascontiguousarray(np.asarray(inputs["L1_attn_ga_w"], np.float32).reshape(8, 128, 1024).transpose(1, 0, 2).reshape(128, 8192))
        W["l1a_bx"] = _colmajor(inputs["L1_attn_b_x"])
        W["l1a_by"] = _colmajor(inputs["L1_attn_b_y"])
        W["l1a_gib"] = _colmajor(inputs["L1_attn_gi_b"])
        W["l1a_gab"] = _colmajor(inputs["L1_attn_ga_b"])
        W["l1a_cb"] = _colmajor(inputs["L1_attn_conv_b"])
        cw = np.asarray(inputs["L1_attn_conv_w"], np.float32)
        W["l1a_cw"] = np.ascontiguousarray(cw.reshape(4, 8, 128).transpose(2, 0, 1).reshape(128, 32))
        W["l1a_ap"] = _colmajor(inputs["L1_attn_a_param"])
    return W


def kernel(**inputs):
    phases = PHASES
    assert np.asarray(inputs["L0_attn_attn_circuit_params"]).shape == (16, 64)
    assert np.asarray(inputs["L0_attn_ffn_circuit_params"]).shape == (16,)
    x = np.asarray(inputs["L0_attn_x"], np.float32)
    xt = np.ascontiguousarray(x[0].T)
    W = _prep_shared(inputs, phases)
    in_maps = []
    for k in range(NCORES):
        d = dict(W)
        d["x_in"] = np.ascontiguousarray(xt[:, k * T:(k + 1) * T])
        d["mcol"] = np.full((128, 1), 0.0 if k == 0 else 1.0, np.float32)
        oh = np.zeros((128, 8), np.float32)
        if k > 0:
            oh[:, k - 1] = 1.0
        d["onehot"] = oh
        d["prev_idx"] = np.array([[(k - 1) % NCORES]], np.uint32)
        in_maps.append(d)
    g = build_program(phases)
    res = run_bass_kernel_spmd(g.nc, in_maps, core_ids=list(range(NCORES)))
    out = np.empty((1, 16384, 1024), np.float32)
    for k in range(NCORES):
        out[0, k * T:(k + 1) * T, :] = res.results[k]["x_out"].T
    return out
